# version 33
# baseline (speedup 1.0000x reference)
import numpy as np
import concourse.bass as bass
import concourse.mybir as mybir
from concourse.bass_utils import run_bass_kernel_spmd

F32 = mybir.dt.float32
BF16 = mybir.dt.bfloat16
AF = mybir.ActivationFunctionType
ALU = mybir.AluOpType
AX = mybir.AxisListType

D = 1024
DI = 2048
KD = D // 128
KI = DI // 128
T = 128
DECAY = 0.9
EPS = 1e-5

N_HALO = 2
N_MAIN = 32
N_CORES = 8


def build_program(n_halo, n_main):
    use_cc = (n_halo == 0)
    NCH = n_halo + n_main
    nc = bass.Bass("TRN2", target_bir_lowering=False)

    x_dr = nc.dram_tensor("x_all", [NCH * T, D], F32, kind="ExternalInput").ap() \
        .rearrange("(c p) d -> c p d", p=T)
    win_dr = nc.dram_tensor("win", [KD, T, DI], F32, kind="ExternalInput").ap()
    wff_dr = nc.dram_tensor("wffnT", [KD, T, D], F32, kind="ExternalInput").ap()
    dsh_dr = nc.dram_tensor("dishat", [KD, T, D], F32, kind="ExternalInput").ap()
    wout_dr = nc.dram_tensor("woutw", [KI, T, D], F32, kind="ExternalInput").ap()
    binb_dr = nc.dram_tensor("bin_b", [T, DI], F32, kind="ExternalInput").ap()
    boutb_dr = nc.dram_tensor("bout_b", [T, D], F32, kind="ExternalInput").ap()
    bp_dr = nc.dram_tensor("bp_cols", [T, KI, 16], F32, kind="ExternalInput").ap()
    cpt_dr = nc.dram_tensor("cpt_cols", [T, KI, 16], F32, kind="ExternalInput").ap()
    dp_dr = nc.dram_tensor("dp_col", [T, KI], F32, kind="ExternalInput").ap()
    gate_dr = nc.dram_tensor("gate", [T, 1], F32, kind="ExternalInput").ap()
    id_dr = nc.dram_tensor("ident", [T, T], BF16, kind="ExternalInput").ap()
    idf_dr = nc.dram_tensor("identf", [T, T], F32, kind="ExternalInput").ap()
    sel_dr = nc.dram_tensor("sel", [T, KI], F32, kind="ExternalInput").ap()
    fixco_dr = nc.dram_tensor("fixco", [T, 8], F32, kind="ExternalInput").ap()
    ones1_dr = nc.dram_tensor("ones1", [1, T], BF16, kind="ExternalInput").ap()
    g_dram = nc.dram_tensor("g_scratch", [KI, T], F32, kind="Internal")
    ag_dram = nc.dram_tensor("ag_scratch", [8 * KI, T], F32, kind="Internal")
    out_dr = nc.dram_tensor("out", [n_main * T, D], F32, kind="ExternalOutput").ap() \
        .rearrange("(c p) d -> c p d", p=T)

    from contextlib import ExitStack
    with ExitStack() as stack:
        sb = lambda n, s, d: stack.enter_context(nc.sbuf_tensor(n, s, d))
        psum = lambda n, s, d: stack.enter_context(nc.psum_tensor(n, s, d))
        sem = lambda n: stack.enter_context(nc.semaphore(n))

        win_bf = sb("win_bf", [T, KD, DI], BF16)
        wff_bf = sb("wff_bf", [T, KD, D], BF16)
        wout_bf = sb("wout_bf", [T, KI, D], BF16)
        binb = sb("binb", [T, DI], F32)
        boutb = sb("boutb", [T, D], F32)
        bp_sb = sb("bp_sb", [T, KI, 16], F32)
        cpt_sb = sb("cpt_sb", [T, KI, 16], F32)
        dp_sb = sb("dp_sb", [T, KI], F32)
        gate_sb = sb("gate_sb", [T, 1], F32)
        id_sb = sb("id_sb", [T, T], BF16)
        dec_sb = sb("dec_sb", [T, T], F32)
        c_col = sb("c_col", [T, KI], F32)
        stage = sb("stage", [T, 2, DI], F32)
        ginit = sb("ginit", [T, KI], F32)
        prodbig = u_sc_alias = None
        idf_sb = sb("idf_sb", [T, T], F32)
        sel_sb = sb("sel_sb", [T, KI], F32)
        fixco_sb = sb("fixco_sb", [T, 8], F32)
        ones1_sb = sb("ones1_sb", [1, T], BF16)
        gout_sb = sb("gout_sb", [KI, T], F32)
        agt_sb = sb("agt_sb", [T, T], F32)
        vb_bf = sb("vb_bf", [T, KI], BF16)
        w0row = sb("w0row", [1, D], BF16)
        x2 = sb("x2", [T, 2, D], F32)
        xn2 = sb("xn2", [T, 2, D], BF16)
        xnf2 = sb("xnf2", [T, 2, D], BF16)
        xit2 = sb("xit2", [T, 2, DI], BF16)
        u_sc = sb("u_sc", [T, DI], F32)
        bin_row = sb("bin_row", [1, DI], BF16)
        cg_sc = sb("cg_sc", [T, DI], F32)
        g2 = sb("g2", [T, 2, KI, T], F32)
        y2 = sb("y2", [T, 2, DI], BF16)
        out2 = sb("out2", [T, 2, D], F32)
        sq_a = sb("sq_a", [T, D], F32)
        stats = sb("stats", [T, 16], F32)

        prodbig = u_sc
        ps_xi = psum("ps_xi", [T, DI], F32)
        ps_aoA = psum("ps_aoA", [T, D], F32)
        ps_aoB = psum("ps_aoB", [T, D], F32)

        s_x = [sem("s_x0"), sem("s_x1")]
        s_sm = sem("s_sm")
        s_id = sem("s_id")
        s_wga = [sem("s_wga0"), sem("s_wga1"), sem("s_wga2"), sem("s_wga3")]
        s_wgb = sem("s_wgb")
        s_w = [sem("s_w0"), sem("s_w1"), sem("s_w2")]
        s_st = [sem("s_st0"), sem("s_st1")]
        s_cc = sem("s_cc")
        s_g = sem("s_g")
        s_f = [sem("s_f0"), sem("s_f1")]
        s_pe = sem("s_pe")
        s_dve = sem("s_dve")
        s_act = sem("s_act")
        s_pool = sem("s_pool")
        SEMS = {"pe": s_pe, "dve": s_dve, "act": s_act, "pool": s_pool}

        def wait_x(engine, c):
            engine.wait_ge(s_x[c % 2], 16 * (c // 2 + 1))

        def wait_st(engine, j):
            if j >= 0:
                engine.wait_ge(s_st[j % 2], 16 * (j // 2 + 1))

        xnf_ps = ps_xi.ap()[:, 0:512].bitcast(BF16)
        xif_psA = ps_aoA.ap().bitcast(BF16)
        xif_psB = ps_aoB.ap().bitcast(BF16)

        def ao_ps(c):
            return ps_aoA.ap() if c % 2 == 0 else ps_aoB.ap()

        def xif_ps(c):
            return xif_psB if c % 2 == 0 else xif_psA

        ismain = lambda c: c >= n_halo

        N_STAGE = 24

        def seq_sync():
            yield ("idload", 0)
            yield ("xload", 0)
            yield ("xload", 1)
            for i in range(10):
                yield ("smload", i)
            for s in range(8, 16):
                yield ("wload", s)
            for c in range(2, NCH):
                yield ("xload", c)

        def seq_pool():
            yield ("windma", 0)
            for d in range(8):
                yield ("ffnadd", d)
                if d >= 4:
                    yield ("woutdma", d - 4)
            if ismain(0):
                yield ("xpb", 0)
            for c in range(NCH):
                if ismain(c + 1) and c + 1 < NCH:
                    yield ("xpb", c + 1)
            if use_cc:
                yield ("gdma", 0)
                yield ("ccall", 0)
                yield ("agload", 0)

        def seq_dve():
            yield ("vhat", 0)
            yield ("xn", 0)
            yield ("ccolA", 0)
            yield ("ccolB", 0)
            yield ("binrow", 0)
            if use_cc:
                for d in range(8):
                    yield ("ffnadd", d)
            for c in range(NCH):
                if c + 1 < NCH:
                    yield ("vhat", c + 1)
                if ismain(c - 1):
                    yield ("oev", c - 1)
                if c + 1 < NCH:
                    yield ("xn", c + 1)
                yield ("scansA", c)
                if ismain(c):
                    yield ("cgvA", c)
                    yield ("yA", c)
                yield ("scansB", c)
                if c == n_halo - 1:
                    yield ("gateg", 0)
                if ismain(c):
                    yield ("cgvB", c)
                    yield ("yB", c)
            yield ("oev", NCH - 1)
            if use_cc:
                yield ("gevac", 0)
                yield ("vmul", 0)
                yield ("w0evac", 0)
                for j in range(7):
                    yield ("fix", j)

        def seq_act():
            yield ("stats", 0)
            yield ("rstd", 0)
            for c in range(NCH):
                yield ("xnfev", c)
                if c + 1 < NCH:
                    yield ("stats", c + 1)
                yield ("silu", c)
                if c + 1 < NCH:
                    yield ("rstd", c + 1)
                if use_cc and c == 1:
                    for i in range(8):
                        yield ("wpa", i)
                if ismain(c - 1):
                    yield ("store", c - 1)
            yield ("store", NCH - 1)
            if use_cc:
                for j in range(7):
                    yield ("fbload", j)
                    yield ("fstore", j)

        def seq_pe():
            yield ("xnT", 0)
            for c in range(NCH):
                yield ("win", c)
                if ismain(c - 1):
                    yield ("woutA", c - 1)
                    yield ("woutB", c - 1)
                if ismain(c):
                    yield ("wffn", c)
                if c + 1 < NCH:
                    yield ("xnT", c + 1)
                yield ("xiTA", c)
                yield ("xiTB", c)
            yield ("woutA", NCH - 1)
            yield ("woutB", NCH - 1)
            if use_cc:
                yield ("gtr", 0)
                yield ("gsel", 0)
                yield ("w0mm", 0)
                yield ("w0bc", 0)

        INCS = {"vhat": 5, "stats": 2, "vmul": 2, "silu": 3}
        ev = {}
        DMA_KINDS = {"store", "fbload", "fstore"}
        for eng, seqf in (("pe", seq_pe), ("dve", seq_dve), ("act", seq_act)):
            cnt = 0
            for ph in seqf():
                if ph[0] in DMA_KINDS:
                    continue
                cnt += INCS.get(ph[0], 1)
                ev[(eng,) + ph] = cnt
        pcnt = 0
        pool_before_xload = {}
        POOL_DMA_KINDS = {"xload", "gdma", "ccall", "agload", "windma", "woutdma"}
        for ph in seq_pool():
            if ph[0] not in POOL_DMA_KINDS:
                pcnt += INCS.get(ph[0], 1)
                ev[("pool",) + ph] = pcnt
            elif ph[0] == "xload":
                pool_before_xload[ph[1]] = pcnt
        ev_w = {}
        wcnt = [0, 0]
        for ph in seq_sync():
            if ph[0] == "wload":
                s = ph[1]
                wcnt[s % 2] += 32
                ev_w[s] = (s % 2, wcnt[s % 2])
        N_SMALL = 160

        def wait_w(engine, s):
            p, v = ev_w[s]
            engine.wait_ge(s_w[p], v)

        def Wt(engine, eng, *key):
            k = (eng,) + key
            if k in ev:
                engine.wait_ge(SEMS[eng], ev[k])

        def selfwait(engine, eng, ph):
            v = ev[(eng,) + ph] - INCS.get(ph[0], 1)
            if v > 0:
                engine.wait_ge(SEMS[eng], v)

        x_tm = lambda c: x2.ap()[:, c % 2, :]
        xpb_tm = lambda c: out2.ap()[:, c % 2, :]
        xn_tm = lambda c: xn2.ap()[:, c % 2, :]
        xnf_sb = lambda c: xnf2.ap()[:, c % 2, :]
        xi_tm = lambda c: xit2.ap()[:, c % 2, :]
        g_sb = lambda c: g2.ap()[:, c % 2, :, :]
        y_sb = lambda c: y2.ap()[:, c % 2, :]
        out_sb = lambda c: out2.ap()[:, c % 2, :]
        stat = lambda c, j: stats.ap()[:, (c % 2) * 8 + j:(c % 2) * 8 + j + 1]

        def emit_sync(sync):
            small = [(binb, binb_dr), (boutb, boutb_dr), (bp_sb, bp_dr),
                     (cpt_sb, cpt_dr), (dp_sb, dp_dr), (gate_sb, gate_dr),
                     (idf_sb, idf_dr), (sel_sb, sel_dr), (fixco_sb, fixco_dr),
                     (ones1_sb, ones1_dr)]
            for ph in seq_sync():
                if ph[0] == "idload":
                    sync.dma_start(out=id_sb.ap()[:], in_=id_dr).then_inc(s_id, 16)
                    continue
                if ph[0] == "smload":
                    dst, src = small[ph[1]]
                    sync.dma_start(out=dst.ap()[:], in_=src).then_inc(s_sm, 16)
                    continue
                if ph[0] == "xload":
                    c = ph[1]
                    if (c - 2) >= 0:
                        Wt(sync, "act", "stats", c - 2)
                        Wt(sync, "dve", "xn", c - 2)
                        Wt(sync, "pool", "xpb", c - 2)
                    sync.dma_start(out=x_tm(c),
                                   in_=x_dr[c]).then_inc(s_x[c % 2], 16)
                    continue
                s = ph[1]
                if s == 8:
                    for q in range(4):
                        sync.wait_ge(s_wga[q], 16)
                p = s - 2
                if 8 <= p < 16:
                    Wt(sync, "dve" if use_cc else "pool", "ffnadd", p - 8)
                buf = stage.ap()[:, s % 2, :]
                sw = s_w[s % 2]
                d = s - 8
                sync.dma_start(out=buf[:, 0:D],
                               in_=wff_dr[d]).then_inc(sw, 16)
                sync.dma_start(out=buf[:, D:2 * D],
                               in_=dsh_dr[d]).then_inc(sw, 16)

        def emit_pool(gpsimd):
            for ph in seq_pool():
                kind, c = ph
                if kind == "windma":
                    wv = win_dr.rearrange("d p n -> p d n")
                    for q in range(4):
                        gpsimd.dma_start(
                            out=win_bf.ap()[:, 2 * q:2 * q + 2, :],
                            in_=wv[:, 2 * q:2 * q + 2, :]).then_inc(s_wga[q], 16)
                elif kind == "woutdma":
                    wv = wout_dr.rearrange("d p n -> p d n")
                    q = c
                    gpsimd.dma_start(
                        out=wout_bf.ap()[:, 4 * q:4 * q + 4, :],
                        in_=wv[:, 4 * q:4 * q + 4, :]).then_inc(s_wgb, 16)
                elif kind == "xpb":
                    selfwait(gpsimd, "pool", ph)
                    wait_x(gpsimd, c)
                    if c == n_halo:
                        gpsimd.wait_ge(s_sm, N_SMALL)
                    wait_st(gpsimd, c - n_halo - 2)
                    Wt(gpsimd, "dve", "oev", c - 2)
                    gpsimd.tensor_tensor(out=xpb_tm(c), in0=x_tm(c),
                                         in1=boutb.ap()[:],
                                         op=ALU.add).then_inc(s_pool, 1)
                elif kind == "ffnadd":
                    selfwait(gpsimd, "pool", ph)
                    d = c
                    wait_w(gpsimd, 8 + d)
                    buf = stage.ap()[:, d % 2, :]
                    gpsimd.tensor_tensor(out=wff_bf.ap()[:, d, :],
                                         in0=buf[:, 0:D], in1=buf[:, D:2 * D],
                                         op=ALU.add).then_inc(s_pool, 1)
                elif kind == "gdma":
                    Wt(gpsimd, "dve", "gevac", 0)
                    gpsimd.dma_start(out=g_dram.ap(), in_=gout_sb.ap()[:]) \
                        .then_inc(s_g, 16)
                elif kind == "ccall":
                    gpsimd.wait_ge(s_g, 16)
                    gpsimd.collective_compute(
                        "AllGather", ALU.bypass,
                        replica_groups=[list(range(N_CORES))],
                        ins=[g_dram.ap().opt()],
                        outs=[ag_dram.ap().opt()]).then_inc(s_cc, 1)
                elif kind == "agload":
                    gpsimd.wait_ge(s_cc, 1)
                    gpsimd.dma_start(out=agt_sb.ap()[:], in_=ag_dram.ap()) \
                        .then_inc(s_g, 32)

        def emit_dve(vector):
            vector.memset(dec_sb.ap()[:], DECAY)
            for ph in seq_dve():
                kind, c = ph
                selfwait(vector, "dve", ph)
                if kind == "ccolA":
                    vector.wait_ge(s_sm, N_SMALL)
                    vector.tensor_tensor(
                        out=prodbig.ap()[:, 0:KI * 16],
                        in0=bp_sb.ap()[:].rearrange("p a b -> p (a b)"),
                        in1=cpt_sb.ap()[:].rearrange("p a b -> p (a b)"),
                        op=ALU.mult).then_inc(s_dve, 1)
                elif kind == "ccolB":
                    last = None
                    for k in range(KI):
                        last = vector.tensor_reduce(
                            out=c_col.ap()[:, k:k + 1],
                            in_=prodbig.ap()[:, k * 16:(k + 1) * 16],
                            axis=AX.X, op=ALU.add)
                    last.then_inc(s_dve, 1)
                elif kind == "ffnadd":
                    d = c
                    wait_w(vector, 8 + d)
                    buf = stage.ap()[:, d % 2, :]
                    vector.tensor_tensor(out=wff_bf.ap()[:, d, :],
                                         in0=buf[:, 0:D], in1=buf[:, D:2 * D],
                                         op=ALU.add).then_inc(s_dve, 1)
                elif kind == "gevac":
                    Wt(vector, "pe", "gtr", 0)
                    vector.tensor_copy(out=gout_sb.ap()[:],
                                       in_=ps_xi.ap()[0:KI, 0:T]) \
                        .then_inc(s_dve, 1)
                elif kind == "vmul":
                    base = ev[("dve",) + ph] - 2
                    Wt(vector, "pe", "gsel", 0)
                    vector.tensor_tensor(out=ginit.ap()[:],
                                         in0=ps_xi.ap()[:, 2 * KI:3 * KI],
                                         in1=c_col.ap()[:],
                                         op=ALU.mult).then_inc(s_dve, 1)
                    vector.wait_ge(s_dve, base + 1)
                    vector.tensor_copy(out=vb_bf.ap()[:],
                                       in_=ginit.ap()[:]).then_inc(s_dve, 1)
                elif kind == "w0evac":
                    Wt(vector, "pe", "w0mm", 0)
                    vector.tensor_copy(out=w0row.ap()[:],
                                       in_=ps_xi.ap()[0:1, 512:512 + D]) \
                        .then_inc(s_dve, 1)
                elif kind == "fix":
                    j = c
                    Wt(vector, "pe", "w0bc", 0)
                    vector.wait_ge(s_f[j % 2], 16 * (j // 2 + 1))
                    vector.scalar_tensor_tensor(
                        out=out2.ap()[:, j % 2, :],
                        in0=ps_aoA.ap()[:],
                        scalar=fixco_sb.ap()[:, j:j + 1],
                        in1=out2.ap()[:, j % 2, :],
                        op0=ALU.mult, op1=ALU.add).then_inc(s_dve, 1)
                elif kind == "binrow":
                    vector.wait_ge(s_sm, N_SMALL)
                    vector.tensor_copy(out=bin_row.ap()[:],
                                       in_=binb.ap()[0:1, :]).then_inc(s_dve, 1)
                elif kind == "vhat":
                    base = ev[("dve",) + ph] - 5
                    Wt(vector, "act", "stats", c)
                    vector.tensor_scalar(out=stat(c, 2), in0=stat(c, 0),
                                         scalar1=-1.0 / D, scalar2=None,
                                         op0=ALU.mult).then_inc(s_dve, 1)
                    vector.tensor_scalar(out=stat(c, 3), in0=stat(c, 1),
                                         scalar1=1.0 / D, scalar2=EPS,
                                         op0=ALU.mult,
                                         op1=ALU.add).then_inc(s_dve, 1)
                    vector.wait_ge(s_dve, base + 2)
                    vector.tensor_tensor(out=stat(c, 4), in0=stat(c, 2),
                                         in1=stat(c, 2),
                                         op=ALU.mult).then_inc(s_dve, 1)
                    vector.wait_ge(s_dve, base + 3)
                    vector.tensor_tensor(out=stat(c, 5), in0=stat(c, 3),
                                         in1=stat(c, 4),
                                         op=ALU.subtract).then_inc(s_dve, 1)
                    vector.wait_ge(s_dve, base + 4)
                    vector.reciprocal(out=stat(c, 6), in_=stat(c, 5)) \
                        .then_inc(s_dve, 1)
                elif kind == "xn":
                    Wt(vector, "act", "rstd", c)
                    Wt(vector, "pe", "xnT", c - 2)
                    vector.tensor_scalar(out=xn_tm(c), in0=x_tm(c),
                                         scalar1=stat(c, 2), scalar2=stat(c, 7),
                                         op0=ALU.add, op1=ALU.mult) \
                        .then_inc(s_dve, 1)
                elif kind == "oev":
                    Wt(vector, "pe", "woutB", c)
                    Wt(vector, "pool", "xpb", c)
                    vector.tensor_tensor(out=out_sb(c), in0=ao_ps(c)[:],
                                         in1=xpb_tm(c), op=ALU.add) \
                        .then_inc(s_dve, 1)
                elif kind in ("scansA", "scansB"):
                    half = 0 if kind == "scansA" else 1
                    Wt(vector, "pe", "xiTA" if half == 0 else "xiTB", c)
                    xif = xif_ps(c)
                    last = None
                    for k in range(half * 8, half * 8 + 8):
                        if c == 0:
                            init = 0.0
                        elif c == n_halo:
                            init = ginit.ap()[:, k:k + 1]
                        else:
                            init = g_sb(c - 1)[:, k, 127:128]
                        last = vector.tensor_tensor_scan(
                            out=g_sb(c)[:, k, :], data0=dec_sb.ap()[:],
                            data1=xif[:, k * T:(k + 1) * T], initial=init,
                            op0=ALU.mult, op1=ALU.add)
                    last.then_inc(s_dve, 1)
                elif kind == "gateg":
                    h = n_halo - 1
                    vector.tensor_scalar_mul(
                        out=ginit.ap()[:, :],
                        in0=g2.ap()[:, h % 2, :, 127],
                        scalar1=gate_sb.ap()[:, 0:1]).then_inc(s_dve, 1)
                elif kind in ("cgvA", "cgvB"):
                    half = 0 if kind == "cgvA" else 1
                    last = None
                    for k in range(half * 8, half * 8 + 8):
                        last = vector.tensor_scalar_mul(
                            out=cg_sc.ap()[:, k * T:(k + 1) * T],
                            in0=g_sb(c)[:, k, :],
                            scalar1=c_col.ap()[:, k:k + 1])
                    last.then_inc(s_dve, 1)
                elif kind in ("yA", "yB"):
                    half = 0 if kind == "yA" else 1
                    Wt(vector, "pe", "woutA" if half == 0 else "woutB", c - 2)
                    xif = xif_ps(c)
                    last = None
                    for k in range(half * 8, half * 8 + 8):
                        last = vector.scalar_tensor_tensor(
                            out=y_sb(c)[:, k * T:(k + 1) * T],
                            in0=xif[:, k * T:(k + 1) * T],
                            scalar=dp_sb.ap()[:, k:k + 1],
                            in1=cg_sc.ap()[:, k * T:(k + 1) * T],
                            op0=ALU.mult, op1=ALU.add)
                    last.then_inc(s_dve, 1)

        def emit_act(scalar):
            for ph in seq_act():
                kind, c = ph
                if kind not in DMA_KINDS:
                    selfwait(scalar, "act", ph)
                if kind == "wpa":
                    i = c
                    wait_w(scalar, 16 + i)
                    buf = stage.ap()[:, (16 + i) % 3, :]
                    scalar.copy(out=wout_bf.ap()[:, i, :], in_=buf[:, 0:D])
                    scalar.copy(out=wout_bf.ap()[:, i + 8, :],
                                in_=buf[:, D:2 * D]).then_inc(s_act, 1)
                elif kind == "fbload":
                    j = c
                    last = n_main - 1 - ((n_main - 1 - (j % 2)) % 2)
                    scalar.wait_ge(s_st[j % 2], 16 * (last // 2 + 1 + j // 2))
                    if j >= 2:
                        Wt(scalar, "dve", "fix", j - 2)
                    scalar.dma_start(out=out2.ap()[:, j % 2, :],
                                     in_=out_dr[j]).then_inc(s_f[j % 2], 16)
                elif kind == "fstore":
                    j = c
                    Wt(scalar, "dve", "fix", j)
                    scalar.dma_start(out=out_dr[j],
                                     in_=out2.ap()[:, j % 2, :]) \
                        .then_inc(s_st[j % 2], 16)
                elif kind == "stats":
                    base = ev[("act",) + ph] - 2
                    wait_x(scalar, c)
                    Wt(scalar, "dve", "vhat", c - 2)
                    scalar.activation(out=sq_a.ap()[:], in_=x_tm(c),
                                      func=AF.Identity,
                                      accum_out=stat(c, 0)).then_inc(s_act, 1)
                    scalar.wait_ge(s_act, base + 1)
                    scalar.activation(out=sq_a.ap()[:], in_=x_tm(c),
                                      func=AF.Square, accum_out=stat(c, 1)) \
                        .then_inc(s_act, 1)
                elif kind == "rstd":
                    Wt(scalar, "dve", "vhat", c)
                    Wt(scalar, "dve", "xn", c - 2)
                    scalar.activation(out=stat(c, 7), in_=stat(c, 6),
                                      func=AF.Sqrt).then_inc(s_act, 1)
                elif kind == "xnfev":
                    Wt(scalar, "pe", "xnT", c)
                    Wt(scalar, "pe", "win", c - 2)
                    scalar.copy(out=xnf_sb(c), in_=xnf_ps[:, 0:D]) \
                        .then_inc(s_act, 1)
                elif kind == "silu":
                    scalar.activation(out=sq_a.ap()[:, 2:3],
                                      in_=sq_a.ap()[:, 0:1],
                                      func=AF.Silu).then_inc(s_act, 1)
                    Wt(scalar, "pe", "win", c)
                    Wt(scalar, "pe", "xiTB", c - 2)
                    scalar.activation(out=xi_tm(c), in_=ps_xi.ap()[:],
                                      func=AF.Silu).then_inc(s_act, 1)
                    scalar.activation(out=sq_a.ap()[:, 3:4],
                                      in_=sq_a.ap()[:, 0:1],
                                      func=AF.Sqrt).then_inc(s_act, 1)
                elif kind == "store":
                    Wt(scalar, "dve", "oev", c)
                    j = c - n_halo
                    scalar.dma_start(out=out_dr[j], in_=out_sb(c)) \
                        .then_inc(s_st[j % 2], 16)

        def emit_pe(tensor):
            tensor.wait_ge(s_id, 16)
            for q in range(4):
                tensor.wait_ge(s_wga[q], 16)
                for _ in range(25):
                    tensor.transpose(out=xif_psA[:, 0:T], in_=id_sb.ap()[:],
                                     identity=id_sb.ap()[:])
            for ph in seq_pe():
                kind, c = ph
                if kind == "xnT":
                    if c == 0:
                        tensor.wait_ge(s_id, 16)
                    Wt(tensor, "act", "silu", c - 1)
                    Wt(tensor, "dve", "xn", c)
                    last = None
                    for d in range(KD):
                        last = tensor.transpose(
                            out=xnf_ps[:, d * T:(d + 1) * T],
                            in_=xn_tm(c)[:, d * T:(d + 1) * T],
                            identity=id_sb.ap()[:])
                    last.then_inc(s_pe, 1)
                elif kind == "win":
                    Wt(tensor, "act", "xnfev", c)
                    if c == 0:
                        for q in range(4):
                            tensor.wait_ge(s_wga[q], 16)
                        Wt(tensor, "dve", "binrow", 0)
                    last = None
                    for d in range(KD):
                        lhsT = xnf_sb(c)[:, d * T:(d + 1) * T]
                        for n in range(4):
                            last = tensor.matmul(
                                out=ps_xi.ap()[:, n * 512:(n + 1) * 512],
                                lhsT=lhsT,
                                rhs=win_bf.ap()[:, d, n * 512:(n + 1) * 512],
                                start=(d == 0), stop=False,
                                skip_group_check=True)
                    for n in range(4):
                        last = tensor.matmul(
                            out=ps_xi.ap()[:, n * 512:(n + 1) * 512],
                            lhsT=ones1_sb.ap()[0:1, :],
                            rhs=bin_row.ap()[0:1, n * 512:(n + 1) * 512],
                            start=False, stop=True, skip_group_check=True)
                    last.then_inc(s_pe, 1)
                elif kind == "wffn":
                    Wt(tensor, "dve", "yB", c - 1)
                    if c == n_halo:
                        Wt(tensor, "dve", "scansB", c - 1)
                        Wt(tensor, "dve" if use_cc else "pool",
                           "ffnadd", KD - 1)
                    last = None
                    for d in range(KD):
                        lhsT = xnf_sb(c)[:, d * T:(d + 1) * T]
                        for n in range(2):
                            last = tensor.matmul(
                                out=ao_ps(c)[:, n * 512:(n + 1) * 512],
                                lhsT=lhsT,
                                rhs=wff_bf.ap()[:, d, n * 512:(n + 1) * 512],
                                start=(d == 0), stop=False,
                                skip_group_check=True)
                    last.then_inc(s_pe, 1)
                elif kind in ("woutA", "woutB"):
                    half = 0 if kind == "woutA" else 1
                    Wt(tensor, "dve", "yA" if half == 0 else "yB", c)
                    if c == n_halo and half == 0:
                        if use_cc:
                            Wt(tensor, "act", "wpa", 7)
                        else:
                            tensor.wait_ge(s_wgb, 64)
                    last = None
                    for k in range(half * 8, half * 8 + 8):
                        lhsT = y_sb(c)[:, k * T:(k + 1) * T]
                        for n in range(2):
                            last = tensor.matmul(
                                out=ao_ps(c)[:, n * 512:(n + 1) * 512],
                                lhsT=lhsT,
                                rhs=wout_bf.ap()[:, k, n * 512:(n + 1) * 512],
                                start=False, stop=(k == KI - 1),
                                skip_group_check=True)
                    last.then_inc(s_pe, 1)
                elif kind == "gtr":
                    Wt(tensor, "dve", "scansB", NCH - 1)
                    gcol = g2.ap()[:, (NCH - 1) % 2, :, 127]
                    tensor.transpose(out=ps_xi.ap()[0:KI, 0:T],
                                     in_=gcol, identity=idf_sb.ap()[:]) \
                        .then_inc(s_pe, 1)
                elif kind == "gsel":
                    tensor.wait_ge(s_g, 48)
                    tensor.matmul(out=ps_xi.ap()[:, 2 * KI:3 * KI],
                                  lhsT=agt_sb.ap()[:], rhs=sel_sb.ap()[:],
                                  start=True, stop=True).then_inc(s_pe, 1)
                elif kind == "w0mm":
                    Wt(tensor, "dve", "vmul", 0)
                    last = None
                    for k in range(KI):
                        for n in range(2):
                            last = tensor.matmul(
                                out=ps_xi.ap()[0:1, 512 + n * 512:512 + (n + 1) * 512],
                                lhsT=vb_bf.ap()[:, k:k + 1],
                                rhs=wout_bf.ap()[:, k, n * 512:(n + 1) * 512],
                                start=(k == 0), stop=(k == KI - 1))
                    last.then_inc(s_pe, 1)
                elif kind == "w0bc":
                    Wt(tensor, "dve", "w0evac", 0)
                    Wt(tensor, "dve", "oev", NCH - 2)
                    last = None
                    for n in range(2):
                        last = tensor.matmul(
                            out=ps_aoA.ap()[:, n * 512:(n + 1) * 512],
                            lhsT=ones1_sb.ap()[:],
                            rhs=w0row.ap()[0:1, n * 512:(n + 1) * 512],
                            start=True, stop=True)
                    last.then_inc(s_pe, 1)
                elif kind in ("xiTA", "xiTB"):
                    half = 0 if kind == "xiTA" else 1
                    Wt(tensor, "act", "silu", c)
                    if half == 0:
                        Wt(tensor, "dve", "oev", c - 1)
                    xif = xif_ps(c)
                    last = None
                    for k in range(half * 8, half * 8 + 8):
                        last = tensor.transpose(
                            out=xif[:, k * T:(k + 1) * T],
                            in_=xi_tm(c)[:, k * T:(k + 1) * T],
                            identity=id_sb.ap()[:])
                    last.then_inc(s_pe, 1)

        block = stack.enter_context(nc.Block())

        @block.sync
        def _(sync):
            emit_sync(sync)

        @block.gpsimd
        def _(gpsimd):
            emit_pool(gpsimd)
            nf0 = 4 if use_cc else 0
            nf1 = 3 if use_cc else 0
            gpsimd.wait_ge(s_st[0], 16 * ((n_main + 1) // 2 + nf0))
            gpsimd.wait_ge(s_st[1], 16 * (n_main // 2 + nf1))

        @block.vector
        def _(vector):
            emit_dve(vector)

        @block.scalar
        def _(scalar):
            emit_act(scalar)

        @block.tensor
        def _(tensor):
            emit_pe(tensor)

    return nc


def make_inputs(x_flat, Win, bin_, Bp, Cp, Dp, Wout, bout, Wffn, disha,
                n_halo, n_main, n_cores, core_tokens, batch_tokens=16384):
    f32 = np.float32
    import ml_dtypes
    win_h = np.ascontiguousarray(Win.reshape(KD, T, DI), dtype=f32)
    wffT = np.ascontiguousarray(Wffn.T.reshape(KD, T, D), dtype=f32)
    dsh = np.ascontiguousarray(np.tile(disha, (D // disha.shape[0], 1))
                               .reshape(KD, T, D), dtype=f32)
    wout_h = np.ascontiguousarray(Wout.reshape(KI, T, D), dtype=f32)
    binb = np.ascontiguousarray(np.broadcast_to(bin_, (T, DI)), dtype=f32)
    boutb = np.ascontiguousarray(np.broadcast_to(bout, (T, D)), dtype=f32)
    bp_c = np.ascontiguousarray(Bp.reshape(KI, T, 16).transpose(1, 0, 2), dtype=f32)
    cpt_c = np.ascontiguousarray(Cp.T.reshape(KI, T, 16).transpose(1, 0, 2), dtype=f32)
    dp_c = np.ascontiguousarray(Dp.reshape(KI, T).T, dtype=f32)
    ident = np.eye(T, dtype=ml_dtypes.bfloat16)
    identf = np.eye(T, dtype=f32)
    ones1 = np.ones((1, T), dtype=ml_dtypes.bfloat16)
    fixco = np.ascontiguousarray(
        (0.9 ** (128.0 * np.arange(8)[None, :] + np.arange(T)[:, None] + 1.0))
        .astype(f32))
    halo_tok = n_halo * T
    in_maps = []
    for k in range(n_cores):
        s0 = k * core_tokens
        seq_start = (s0 % batch_tokens == 0)
        gate = np.zeros((T, 1), f32) if seq_start else np.ones((T, 1), f32)
        sel = np.zeros((T, KI), dtype=f32)
        if not seq_start:
            r = k - 1
            for kcol in range(KI):
                sel[r * KI + kcol, kcol] = 1.0
        if n_halo > 0:
            halo = (np.zeros((halo_tok, D), dtype=f32) if seq_start
                    else x_flat[s0 - halo_tok:s0])
            x_all = np.ascontiguousarray(
                np.concatenate([halo, x_flat[s0:s0 + core_tokens]], axis=0),
                dtype=f32)
        else:
            x_all = np.ascontiguousarray(x_flat[s0:s0 + core_tokens], dtype=f32)
        in_maps.append({
            "x_all": x_all, "win": win_h, "wffnT": wffT, "dishat": dsh,
            "woutw": wout_h, "bin_b": binb, "bout_b": boutb,
            "bp_cols": bp_c, "cpt_cols": cpt_c, "dp_col": dp_c,
            "gate": gate, "ident": ident, "identf": identf, "sel": sel,
            "fixco": fixco, "ones1": ones1,
        })
    return in_maps


_PROGRAM_CACHE = {}


def kernel(L0_attn_x, L0_attn_Win, L0_attn_bin_, L0_attn_Bp, L0_attn_Cp,
           L0_attn_Dp, L0_attn_Wout, L0_attn_bout, L0_ffn_weight,
           L0_ffn_disha):
    x = np.ascontiguousarray(np.asarray(L0_attn_x, dtype=np.float32)
                             .reshape(2 * 16384, D))
    core_tokens = N_MAIN * T
    in_maps = make_inputs(
        x, np.asarray(L0_attn_Win, dtype=np.float32),
        np.asarray(L0_attn_bin_, dtype=np.float32),
        np.asarray(L0_attn_Bp, dtype=np.float32),
        np.asarray(L0_attn_Cp, dtype=np.float32),
        np.asarray(L0_attn_Dp, dtype=np.float32),
        np.asarray(L0_attn_Wout, dtype=np.float32),
        np.asarray(L0_attn_bout, dtype=np.float32),
        np.asarray(L0_ffn_weight, dtype=np.float32),
        np.asarray(L0_ffn_disha, dtype=np.float32),
        N_HALO, N_MAIN, N_CORES, core_tokens)
    key = (N_HALO, N_MAIN)
    if key not in _PROGRAM_CACHE:
        _PROGRAM_CACHE[key] = build_program(N_HALO, N_MAIN)
    nc = _PROGRAM_CACHE[key]
    res = run_bass_kernel_spmd(nc, in_maps, core_ids=list(range(N_CORES)))
    out = np.concatenate([res.results[k]["out"] for k in range(N_CORES)],
                         axis=0)
    return out.reshape(2, 16384, D)


# version 34
# speedup vs baseline: 1.0587x; 1.0587x over previous
import numpy as np
import concourse.bass as bass
import concourse.mybir as mybir
from concourse.bass_utils import run_bass_kernel_spmd

F32 = mybir.dt.float32
BF16 = mybir.dt.bfloat16
AF = mybir.ActivationFunctionType
ALU = mybir.AluOpType
AX = mybir.AxisListType

D = 1024
DI = 2048
KD = D // 128
KI = DI // 128
T = 128
DECAY = 0.9
EPS = 1e-5

N_HALO = 2
N_MAIN = 32
N_CORES = 8


def build_program(n_halo, n_main):
    use_cc = (n_halo == 0)
    NCH = n_halo + n_main
    nc = bass.Bass("TRN2", target_bir_lowering=False)

    x_dr = nc.dram_tensor("x_all", [NCH * T, D], F32, kind="ExternalInput").ap() \
        .rearrange("(c p) d -> c p d", p=T)
    win_dr = nc.dram_tensor("win", [KD, T, DI], F32, kind="ExternalInput").ap()
    wff_dr = nc.dram_tensor("wffnT", [KD, T, D], F32, kind="ExternalInput").ap()
    dsh_dr = nc.dram_tensor("dishat", [KD, T, D], F32, kind="ExternalInput").ap()
    wout_dr = nc.dram_tensor("woutw", [KI, T, D], F32, kind="ExternalInput").ap()
    binb_dr = nc.dram_tensor("bin_b", [T, DI], F32, kind="ExternalInput").ap()
    boutb_dr = nc.dram_tensor("bout_b", [T, D], F32, kind="ExternalInput").ap()
    bp_dr = nc.dram_tensor("bp_cols", [T, KI, 16], F32, kind="ExternalInput").ap()
    cpt_dr = nc.dram_tensor("cpt_cols", [T, KI, 16], F32, kind="ExternalInput").ap()
    dp_dr = nc.dram_tensor("dp_col", [T, KI], F32, kind="ExternalInput").ap()
    gate_dr = nc.dram_tensor("gate", [T, 1], F32, kind="ExternalInput").ap()
    id_dr = nc.dram_tensor("ident", [T, T], BF16, kind="ExternalInput").ap()
    idf_dr = nc.dram_tensor("identf", [T, T], F32, kind="ExternalInput").ap()
    sel_dr = nc.dram_tensor("sel", [T, KI], F32, kind="ExternalInput").ap()
    fixco_dr = nc.dram_tensor("fixco", [T, 8], F32, kind="ExternalInput").ap()
    ones1_dr = nc.dram_tensor("ones1", [1, T], BF16, kind="ExternalInput").ap()
    g_dram = nc.dram_tensor("g_scratch", [KI, T], F32, kind="Internal")
    ag_dram = nc.dram_tensor("ag_scratch", [8 * KI, T], F32, kind="Internal")
    out_dr = nc.dram_tensor("out", [n_main * T, D], F32, kind="ExternalOutput").ap() \
        .rearrange("(c p) d -> c p d", p=T)

    from contextlib import ExitStack
    with ExitStack() as stack:
        sb = lambda n, s, d: stack.enter_context(nc.sbuf_tensor(n, s, d))
        psum = lambda n, s, d: stack.enter_context(nc.psum_tensor(n, s, d))
        sem = lambda n: stack.enter_context(nc.semaphore(n))

        win_bf = sb("win_bf", [T, KD, DI], BF16)
        wff_bf = sb("wff_bf", [T, KD, D], BF16)
        wout_bf = sb("wout_bf", [T, KI, D], BF16)
        binb = sb("binb", [T, DI], F32)
        boutb = sb("boutb", [T, D], F32)
        bp_sb = sb("bp_sb", [T, KI, 16], F32)
        cpt_sb = sb("cpt_sb", [T, KI, 16], F32)
        dp_sb = sb("dp_sb", [T, KI], F32)
        gate_sb = sb("gate_sb", [T, 1], F32)
        id_sb = sb("id_sb", [T, T], BF16)
        dec_sb = sb("dec_sb", [T, T], F32)
        c_col = sb("c_col", [T, KI], F32)
        stage = sb("stage", [T, 2, DI], F32)
        ginit = sb("ginit", [T, KI], F32)
        prodbig = u_sc_alias = None
        idf_sb = sb("idf_sb", [T, T], F32)
        sel_sb = sb("sel_sb", [T, KI], F32)
        fixco_sb = sb("fixco_sb", [T, 8], F32)
        ones1_sb = sb("ones1_sb", [1, T], BF16)
        gout_sb = sb("gout_sb", [KI, T], F32)
        agt_sb = sb("agt_sb", [T, T], F32)
        vb_bf = sb("vb_bf", [T, KI], BF16)
        w0row = sb("w0row", [1, D], BF16)
        x2 = sb("x2", [T, 2, D], F32)
        xn2 = sb("xn2", [T, 2, D], BF16)
        xnf2 = sb("xnf2", [T, 2, D], BF16)
        xit2 = sb("xit2", [T, 2, DI], BF16)
        u_sc = sb("u_sc", [T, DI], F32)
        bin_row = sb("bin_row", [1, DI], BF16)
        cg_sc = sb("cg_sc", [T, DI], F32)
        g2 = sb("g2", [T, 2, KI, T], F32)
        y2 = sb("y2", [T, 2, DI], BF16)
        out2 = sb("out2", [T, 2, D], F32)
        sq_a = sb("sq_a", [T, D], F32)
        stats = sb("stats", [T, 16], F32)

        prodbig = u_sc
        ps_xi = psum("ps_xi", [T, DI], F32)
        ps_aoA = psum("ps_aoA", [T, D], F32)
        ps_aoB = psum("ps_aoB", [T, D], F32)

        s_x = [sem("s_x0"), sem("s_x1")]
        s_sm = sem("s_sm")
        s_id = sem("s_id")
        s_wga = [sem("s_wga0"), sem("s_wga1"), sem("s_wga2"), sem("s_wga3")]
        s_wgb = sem("s_wgb")
        s_w = [sem("s_w0"), sem("s_w1"), sem("s_w2")]
        s_st = [sem("s_st0"), sem("s_st1")]
        s_cc = sem("s_cc")
        s_g = sem("s_g")
        s_f = [sem("s_f0"), sem("s_f1")]
        s_pe = sem("s_pe")
        s_dve = sem("s_dve")
        s_act = sem("s_act")
        s_pool = sem("s_pool")
        SEMS = {"pe": s_pe, "dve": s_dve, "act": s_act, "pool": s_pool}

        def wait_x(engine, c):
            engine.wait_ge(s_x[c % 2], 16 * (c // 2 + 1))

        def wait_st(engine, j):
            if j >= 0:
                engine.wait_ge(s_st[j % 2], 16 * (j // 2 + 1))

        xnf_ps = ps_xi.ap()[:, 0:512].bitcast(BF16)
        xif_psA = ps_aoA.ap().bitcast(BF16)
        xif_psB = ps_aoB.ap().bitcast(BF16)

        def ao_ps(c):
            return ps_aoA.ap() if c % 2 == 0 else ps_aoB.ap()

        def xif_ps(c):
            return xif_psB if c % 2 == 0 else xif_psA

        ismain = lambda c: c >= n_halo

        N_STAGE = 24

        def seq_sync():
            yield ("idload", 0)
            yield ("xload", 0)
            yield ("xload", 1)
            for i in range(10):
                yield ("smload", i)
            for s in range(8, 16):
                yield ("wload", s)
            for c in range(2, NCH):
                yield ("xload", c)

        def seq_pool():
            yield ("windma", 0)
            for d in range(8):
                yield ("ffnadd", d)
                if d >= 4:
                    yield ("woutdma", d - 4)
            if ismain(0):
                yield ("xpb", 0)
            for c in range(NCH):
                if ismain(c + 1) and c + 1 < NCH:
                    yield ("xpb", c + 1)
            if use_cc:
                yield ("gdma", 0)
                yield ("ccall", 0)
                yield ("agload", 0)

        def seq_dve():
            yield ("vhat", 0)
            yield ("xn", 0)
            yield ("ccolA", 0)
            yield ("ccolB", 0)
            yield ("binrow", 0)
            if use_cc:
                for d in range(8):
                    yield ("ffnadd", d)
            for c in range(NCH):
                if c + 1 < NCH:
                    yield ("vhat", c + 1)
                if ismain(c - 1):
                    yield ("oev", c - 1)
                if c + 1 < NCH:
                    yield ("xn", c + 1)
                yield ("scansA", c)
                if ismain(c):
                    yield ("cgvA", c)
                    yield ("yA", c)
                yield ("scansB", c)
                if c == n_halo - 1:
                    yield ("gateg", 0)
                if ismain(c):
                    yield ("cgvB", c)
                    yield ("yB", c)
            yield ("oev", NCH - 1)
            if use_cc:
                yield ("gevac", 0)
                yield ("vmul", 0)
                yield ("w0evac", 0)
                for j in range(7):
                    yield ("fix", j)

        def seq_act():
            yield ("stats", 0)
            yield ("rstd", 0)
            for c in range(NCH):
                yield ("xnfev", c)
                if c + 1 < NCH:
                    yield ("stats", c + 1)
                yield ("silu", c)
                if c + 1 < NCH:
                    yield ("rstd", c + 1)
                if use_cc and c == 1:
                    for i in range(8):
                        yield ("wpa", i)
                if ismain(c - 1):
                    yield ("store", c - 1)
            yield ("store", NCH - 1)
            if use_cc:
                for j in range(7):
                    yield ("fbload", j)
                    yield ("fstore", j)

        def seq_pe():
            yield ("xnT", 0)
            for c in range(NCH):
                yield ("win", c)
                if ismain(c - 1):
                    yield ("woutA", c - 1)
                    yield ("woutB", c - 1)
                if ismain(c):
                    yield ("wffn", c)
                yield ("xiTA", c)
                if c + 1 < NCH:
                    yield ("xnT", c + 1)
                yield ("xiTB", c)
            yield ("woutA", NCH - 1)
            yield ("woutB", NCH - 1)
            if use_cc:
                yield ("gtr", 0)
                yield ("gsel", 0)
                yield ("w0mm", 0)
                yield ("w0bc", 0)

        INCS = {"vhat": 5, "stats": 2, "vmul": 2, "silu": 3}
        ev = {}
        DMA_KINDS = {"store", "fbload", "fstore"}
        for eng, seqf in (("pe", seq_pe), ("dve", seq_dve), ("act", seq_act)):
            cnt = 0
            for ph in seqf():
                if ph[0] in DMA_KINDS:
                    continue
                cnt += INCS.get(ph[0], 1)
                ev[(eng,) + ph] = cnt
        pcnt = 0
        pool_before_xload = {}
        POOL_DMA_KINDS = {"xload", "gdma", "ccall", "agload", "windma", "woutdma"}
        for ph in seq_pool():
            if ph[0] not in POOL_DMA_KINDS:
                pcnt += INCS.get(ph[0], 1)
                ev[("pool",) + ph] = pcnt
            elif ph[0] == "xload":
                pool_before_xload[ph[1]] = pcnt
        ev_w = {}
        wcnt = [0, 0]
        for ph in seq_sync():
            if ph[0] == "wload":
                s = ph[1]
                wcnt[s % 2] += 32
                ev_w[s] = (s % 2, wcnt[s % 2])
        N_SMALL = 160

        def wait_w(engine, s):
            p, v = ev_w[s]
            engine.wait_ge(s_w[p], v)

        def Wt(engine, eng, *key):
            k = (eng,) + key
            if k in ev:
                engine.wait_ge(SEMS[eng], ev[k])

        def selfwait(engine, eng, ph):
            v = ev[(eng,) + ph] - INCS.get(ph[0], 1)
            if v > 0:
                engine.wait_ge(SEMS[eng], v)

        x_tm = lambda c: x2.ap()[:, c % 2, :]
        xpb_tm = lambda c: out2.ap()[:, c % 2, :]
        xn_tm = lambda c: xn2.ap()[:, c % 2, :]
        xnf_sb = lambda c: xnf2.ap()[:, c % 2, :]
        xi_tm = lambda c: xit2.ap()[:, c % 2, :]
        g_sb = lambda c: g2.ap()[:, c % 2, :, :]
        y_sb = lambda c: y2.ap()[:, c % 2, :]
        out_sb = lambda c: out2.ap()[:, c % 2, :]
        stat = lambda c, j: stats.ap()[:, (c % 2) * 8 + j:(c % 2) * 8 + j + 1]

        def emit_sync(sync):
            small = [(binb, binb_dr), (boutb, boutb_dr), (bp_sb, bp_dr),
                     (cpt_sb, cpt_dr), (dp_sb, dp_dr), (gate_sb, gate_dr),
                     (idf_sb, idf_dr), (sel_sb, sel_dr), (fixco_sb, fixco_dr),
                     (ones1_sb, ones1_dr)]
            for ph in seq_sync():
                if ph[0] == "idload":
                    sync.dma_start(out=id_sb.ap()[:], in_=id_dr).then_inc(s_id, 16)
                    continue
                if ph[0] == "smload":
                    dst, src = small[ph[1]]
                    sync.dma_start(out=dst.ap()[:], in_=src).then_inc(s_sm, 16)
                    continue
                if ph[0] == "xload":
                    c = ph[1]
                    if (c - 2) >= 0:
                        Wt(sync, "act", "stats", c - 2)
                        Wt(sync, "dve", "xn", c - 2)
                        Wt(sync, "pool", "xpb", c - 2)
                    sync.dma_start(out=x_tm(c),
                                   in_=x_dr[c]).then_inc(s_x[c % 2], 16)
                    continue
                s = ph[1]
                if s == 8:
                    for q in range(4):
                        sync.wait_ge(s_wga[q], 16)
                p = s - 2
                if 8 <= p < 16:
                    Wt(sync, "dve" if use_cc else "pool", "ffnadd", p - 8)
                buf = stage.ap()[:, s % 2, :]
                sw = s_w[s % 2]
                d = s - 8
                sync.dma_start(out=buf[:, 0:D],
                               in_=wff_dr[d]).then_inc(sw, 16)
                sync.dma_start(out=buf[:, D:2 * D],
                               in_=dsh_dr[d]).then_inc(sw, 16)

        def emit_pool(gpsimd):
            for ph in seq_pool():
                kind, c = ph
                if kind == "windma":
                    wv = win_dr.rearrange("d p n -> p d n")
                    for q in range(4):
                        gpsimd.dma_start(
                            out=win_bf.ap()[:, 2 * q:2 * q + 2, :],
                            in_=wv[:, 2 * q:2 * q + 2, :]).then_inc(s_wga[q], 16)
                elif kind == "woutdma":
                    wv = wout_dr.rearrange("d p n -> p d n")
                    q = c
                    gpsimd.dma_start(
                        out=wout_bf.ap()[:, 4 * q:4 * q + 4, :],
                        in_=wv[:, 4 * q:4 * q + 4, :]).then_inc(s_wgb, 16)
                elif kind == "xpb":
                    selfwait(gpsimd, "pool", ph)
                    wait_x(gpsimd, c)
                    if c == n_halo:
                        gpsimd.wait_ge(s_sm, N_SMALL)
                    wait_st(gpsimd, c - n_halo - 2)
                    Wt(gpsimd, "dve", "oev", c - 2)
                    gpsimd.tensor_tensor(out=xpb_tm(c), in0=x_tm(c),
                                         in1=boutb.ap()[:],
                                         op=ALU.add).then_inc(s_pool, 1)
                elif kind == "ffnadd":
                    selfwait(gpsimd, "pool", ph)
                    d = c
                    wait_w(gpsimd, 8 + d)
                    buf = stage.ap()[:, d % 2, :]
                    gpsimd.tensor_tensor(out=wff_bf.ap()[:, d, :],
                                         in0=buf[:, 0:D], in1=buf[:, D:2 * D],
                                         op=ALU.add).then_inc(s_pool, 1)
                elif kind == "gdma":
                    Wt(gpsimd, "dve", "gevac", 0)
                    gpsimd.dma_start(out=g_dram.ap(), in_=gout_sb.ap()[:]) \
                        .then_inc(s_g, 16)
                elif kind == "ccall":
                    gpsimd.wait_ge(s_g, 16)
                    gpsimd.collective_compute(
                        "AllGather", ALU.bypass,
                        replica_groups=[list(range(N_CORES))],
                        ins=[g_dram.ap().opt()],
                        outs=[ag_dram.ap().opt()]).then_inc(s_cc, 1)
                elif kind == "agload":
                    gpsimd.wait_ge(s_cc, 1)
                    gpsimd.dma_start(out=agt_sb.ap()[:], in_=ag_dram.ap()) \
                        .then_inc(s_g, 32)

        def emit_dve(vector):
            vector.memset(dec_sb.ap()[:], DECAY)
            for ph in seq_dve():
                kind, c = ph
                selfwait(vector, "dve", ph)
                if kind == "ccolA":
                    vector.wait_ge(s_sm, N_SMALL)
                    vector.tensor_tensor(
                        out=prodbig.ap()[:, 0:KI * 16],
                        in0=bp_sb.ap()[:].rearrange("p a b -> p (a b)"),
                        in1=cpt_sb.ap()[:].rearrange("p a b -> p (a b)"),
                        op=ALU.mult).then_inc(s_dve, 1)
                elif kind == "ccolB":
                    last = None
                    for k in range(KI):
                        last = vector.tensor_reduce(
                            out=c_col.ap()[:, k:k + 1],
                            in_=prodbig.ap()[:, k * 16:(k + 1) * 16],
                            axis=AX.X, op=ALU.add)
                    last.then_inc(s_dve, 1)
                elif kind == "ffnadd":
                    d = c
                    wait_w(vector, 8 + d)
                    buf = stage.ap()[:, d % 2, :]
                    vector.tensor_tensor(out=wff_bf.ap()[:, d, :],
                                         in0=buf[:, 0:D], in1=buf[:, D:2 * D],
                                         op=ALU.add).then_inc(s_dve, 1)
                elif kind == "gevac":
                    Wt(vector, "pe", "gtr", 0)
                    vector.tensor_copy(out=gout_sb.ap()[:],
                                       in_=ps_xi.ap()[0:KI, 0:T]) \
                        .then_inc(s_dve, 1)
                elif kind == "vmul":
                    base = ev[("dve",) + ph] - 2
                    Wt(vector, "pe", "gsel", 0)
                    vector.tensor_tensor(out=ginit.ap()[:],
                                         in0=ps_xi.ap()[:, 2 * KI:3 * KI],
                                         in1=c_col.ap()[:],
                                         op=ALU.mult).then_inc(s_dve, 1)
                    vector.wait_ge(s_dve, base + 1)
                    vector.tensor_copy(out=vb_bf.ap()[:],
                                       in_=ginit.ap()[:]).then_inc(s_dve, 1)
                elif kind == "w0evac":
                    Wt(vector, "pe", "w0mm", 0)
                    vector.tensor_copy(out=w0row.ap()[:],
                                       in_=ps_xi.ap()[0:1, 512:512 + D]) \
                        .then_inc(s_dve, 1)
                elif kind == "fix":
                    j = c
                    Wt(vector, "pe", "w0bc", 0)
                    vector.wait_ge(s_f[j % 2], 16 * (j // 2 + 1))
                    vector.scalar_tensor_tensor(
                        out=out2.ap()[:, j % 2, :],
                        in0=ps_aoA.ap()[:],
                        scalar=fixco_sb.ap()[:, j:j + 1],
                        in1=out2.ap()[:, j % 2, :],
                        op0=ALU.mult, op1=ALU.add).then_inc(s_dve, 1)
                elif kind == "binrow":
                    vector.wait_ge(s_sm, N_SMALL)
                    vector.tensor_copy(out=bin_row.ap()[:],
                                       in_=binb.ap()[0:1, :]).then_inc(s_dve, 1)
                elif kind == "vhat":
                    base = ev[("dve",) + ph] - 5
                    Wt(vector, "act", "stats", c)
                    vector.tensor_scalar(out=stat(c, 2), in0=stat(c, 0),
                                         scalar1=-1.0 / D, scalar2=None,
                                         op0=ALU.mult).then_inc(s_dve, 1)
                    vector.tensor_scalar(out=stat(c, 3), in0=stat(c, 1),
                                         scalar1=1.0 / D, scalar2=EPS,
                                         op0=ALU.mult,
                                         op1=ALU.add).then_inc(s_dve, 1)
                    vector.wait_ge(s_dve, base + 2)
                    vector.tensor_tensor(out=stat(c, 4), in0=stat(c, 2),
                                         in1=stat(c, 2),
                                         op=ALU.mult).then_inc(s_dve, 1)
                    vector.wait_ge(s_dve, base + 3)
                    vector.tensor_tensor(out=stat(c, 5), in0=stat(c, 3),
                                         in1=stat(c, 4),
                                         op=ALU.subtract).then_inc(s_dve, 1)
                    vector.wait_ge(s_dve, base + 4)
                    vector.reciprocal(out=stat(c, 6), in_=stat(c, 5)) \
                        .then_inc(s_dve, 1)
                elif kind == "xn":
                    Wt(vector, "act", "rstd", c)
                    Wt(vector, "pe", "xnT", c - 2)
                    vector.tensor_scalar(out=xn_tm(c), in0=x_tm(c),
                                         scalar1=stat(c, 2), scalar2=stat(c, 7),
                                         op0=ALU.add, op1=ALU.mult) \
                        .then_inc(s_dve, 1)
                elif kind == "oev":
                    Wt(vector, "pe", "woutB", c)
                    Wt(vector, "pool", "xpb", c)
                    vector.tensor_tensor(out=out_sb(c), in0=ao_ps(c)[:],
                                         in1=xpb_tm(c), op=ALU.add) \
                        .then_inc(s_dve, 1)
                elif kind in ("scansA", "scansB"):
                    half = 0 if kind == "scansA" else 1
                    Wt(vector, "pe", "xiTA" if half == 0 else "xiTB", c)
                    xif = xif_ps(c)
                    last = None
                    for k in range(half * 8, half * 8 + 8):
                        if c == 0:
                            init = 0.0
                        elif c == n_halo:
                            init = ginit.ap()[:, k:k + 1]
                        else:
                            init = g_sb(c - 1)[:, k, 127:128]
                        last = vector.tensor_tensor_scan(
                            out=g_sb(c)[:, k, :], data0=dec_sb.ap()[:],
                            data1=xif[:, k * T:(k + 1) * T], initial=init,
                            op0=ALU.mult, op1=ALU.add)
                    last.then_inc(s_dve, 1)
                elif kind == "gateg":
                    h = n_halo - 1
                    vector.tensor_scalar_mul(
                        out=ginit.ap()[:, :],
                        in0=g2.ap()[:, h % 2, :, 127],
                        scalar1=gate_sb.ap()[:, 0:1]).then_inc(s_dve, 1)
                elif kind in ("cgvA", "cgvB"):
                    half = 0 if kind == "cgvA" else 1
                    last = None
                    for k in range(half * 8, half * 8 + 8):
                        last = vector.tensor_scalar_mul(
                            out=cg_sc.ap()[:, k * T:(k + 1) * T],
                            in0=g_sb(c)[:, k, :],
                            scalar1=c_col.ap()[:, k:k + 1])
                    last.then_inc(s_dve, 1)
                elif kind in ("yA", "yB"):
                    half = 0 if kind == "yA" else 1
                    Wt(vector, "pe", "woutA" if half == 0 else "woutB", c - 2)
                    xif = xif_ps(c)
                    last = None
                    for k in range(half * 8, half * 8 + 8):
                        last = vector.scalar_tensor_tensor(
                            out=y_sb(c)[:, k * T:(k + 1) * T],
                            in0=xif[:, k * T:(k + 1) * T],
                            scalar=dp_sb.ap()[:, k:k + 1],
                            in1=cg_sc.ap()[:, k * T:(k + 1) * T],
                            op0=ALU.mult, op1=ALU.add)
                    last.then_inc(s_dve, 1)

        def emit_act(scalar):
            for ph in seq_act():
                kind, c = ph
                if kind not in DMA_KINDS:
                    selfwait(scalar, "act", ph)
                if kind == "wpa":
                    i = c
                    wait_w(scalar, 16 + i)
                    buf = stage.ap()[:, (16 + i) % 3, :]
                    scalar.copy(out=wout_bf.ap()[:, i, :], in_=buf[:, 0:D])
                    scalar.copy(out=wout_bf.ap()[:, i + 8, :],
                                in_=buf[:, D:2 * D]).then_inc(s_act, 1)
                elif kind == "fbload":
                    j = c
                    last = n_main - 1 - ((n_main - 1 - (j % 2)) % 2)
                    scalar.wait_ge(s_st[j % 2], 16 * (last // 2 + 1 + j // 2))
                    if j >= 2:
                        Wt(scalar, "dve", "fix", j - 2)
                    scalar.dma_start(out=out2.ap()[:, j % 2, :],
                                     in_=out_dr[j]).then_inc(s_f[j % 2], 16)
                elif kind == "fstore":
                    j = c
                    Wt(scalar, "dve", "fix", j)
                    scalar.dma_start(out=out_dr[j],
                                     in_=out2.ap()[:, j % 2, :]) \
                        .then_inc(s_st[j % 2], 16)
                elif kind == "stats":
                    base = ev[("act",) + ph] - 2
                    wait_x(scalar, c)
                    Wt(scalar, "dve", "vhat", c - 2)
                    scalar.activation(out=sq_a.ap()[:], in_=x_tm(c),
                                      func=AF.Identity,
                                      accum_out=stat(c, 0)).then_inc(s_act, 1)
                    scalar.wait_ge(s_act, base + 1)
                    scalar.activation(out=sq_a.ap()[:], in_=x_tm(c),
                                      func=AF.Square, accum_out=stat(c, 1)) \
                        .then_inc(s_act, 1)
                elif kind == "rstd":
                    Wt(scalar, "dve", "vhat", c)
                    Wt(scalar, "dve", "xn", c - 2)
                    scalar.activation(out=stat(c, 7), in_=stat(c, 6),
                                      func=AF.Sqrt).then_inc(s_act, 1)
                elif kind == "xnfev":
                    Wt(scalar, "pe", "xnT", c)
                    Wt(scalar, "pe", "win", c - 2)
                    scalar.copy(out=xnf_sb(c), in_=xnf_ps[:, 0:D]) \
                        .then_inc(s_act, 1)
                elif kind == "silu":
                    scalar.activation(out=sq_a.ap()[:, 2:3],
                                      in_=sq_a.ap()[:, 0:1],
                                      func=AF.Silu).then_inc(s_act, 1)
                    Wt(scalar, "pe", "win", c)
                    Wt(scalar, "pe", "xiTB", c - 2)
                    scalar.activation(out=xi_tm(c), in_=ps_xi.ap()[:],
                                      func=AF.Silu).then_inc(s_act, 1)
                    scalar.activation(out=sq_a.ap()[:, 3:4],
                                      in_=sq_a.ap()[:, 0:1],
                                      func=AF.Sqrt).then_inc(s_act, 1)
                elif kind == "store":
                    Wt(scalar, "dve", "oev", c)
                    j = c - n_halo
                    scalar.dma_start(out=out_dr[j], in_=out_sb(c)) \
                        .then_inc(s_st[j % 2], 16)

        def emit_pe(tensor):
            tensor.wait_ge(s_id, 16)
            for q in range(4):
                tensor.wait_ge(s_wga[q], 16)
                for _ in range(25):
                    tensor.transpose(out=xif_psA[:, 0:T], in_=id_sb.ap()[:],
                                     identity=id_sb.ap()[:])
            for ph in seq_pe():
                kind, c = ph
                if kind == "xnT":
                    if c == 0:
                        tensor.wait_ge(s_id, 16)
                    Wt(tensor, "act", "silu", c - 1)
                    Wt(tensor, "dve", "xn", c)
                    last = None
                    for d in range(KD):
                        last = tensor.transpose(
                            out=xnf_ps[:, d * T:(d + 1) * T],
                            in_=xn_tm(c)[:, d * T:(d + 1) * T],
                            identity=id_sb.ap()[:])
                    last.then_inc(s_pe, 1)
                elif kind == "win":
                    Wt(tensor, "act", "xnfev", c)
                    if c == 0:
                        for q in range(4):
                            tensor.wait_ge(s_wga[q], 16)
                        Wt(tensor, "dve", "binrow", 0)
                    last = None
                    for d in range(KD):
                        lhsT = xnf_sb(c)[:, d * T:(d + 1) * T]
                        for n in range(4):
                            last = tensor.matmul(
                                out=ps_xi.ap()[:, n * 512:(n + 1) * 512],
                                lhsT=lhsT,
                                rhs=win_bf.ap()[:, d, n * 512:(n + 1) * 512],
                                start=(d == 0), stop=False,
                                skip_group_check=True)
                    for n in range(4):
                        last = tensor.matmul(
                            out=ps_xi.ap()[:, n * 512:(n + 1) * 512],
                            lhsT=ones1_sb.ap()[0:1, :],
                            rhs=bin_row.ap()[0:1, n * 512:(n + 1) * 512],
                            start=False, stop=True, skip_group_check=True)
                    last.then_inc(s_pe, 1)
                elif kind == "wffn":
                    Wt(tensor, "dve", "yB", c - 1)
                    if c == n_halo:
                        Wt(tensor, "dve", "scansB", c - 1)
                        Wt(tensor, "dve" if use_cc else "pool",
                           "ffnadd", KD - 1)
                    last = None
                    for d in range(KD):
                        lhsT = xnf_sb(c)[:, d * T:(d + 1) * T]
                        for n in range(2):
                            last = tensor.matmul(
                                out=ao_ps(c)[:, n * 512:(n + 1) * 512],
                                lhsT=lhsT,
                                rhs=wff_bf.ap()[:, d, n * 512:(n + 1) * 512],
                                start=(d == 0), stop=False,
                                skip_group_check=True)
                    last.then_inc(s_pe, 1)
                elif kind in ("woutA", "woutB"):
                    half = 0 if kind == "woutA" else 1
                    Wt(tensor, "dve", "yA" if half == 0 else "yB", c)
                    if c == n_halo and half == 0:
                        if use_cc:
                            Wt(tensor, "act", "wpa", 7)
                        else:
                            tensor.wait_ge(s_wgb, 64)
                    last = None
                    for k in range(half * 8, half * 8 + 8):
                        lhsT = y_sb(c)[:, k * T:(k + 1) * T]
                        for n in range(2):
                            last = tensor.matmul(
                                out=ao_ps(c)[:, n * 512:(n + 1) * 512],
                                lhsT=lhsT,
                                rhs=wout_bf.ap()[:, k, n * 512:(n + 1) * 512],
                                start=False, stop=(k == KI - 1),
                                skip_group_check=True)
                    last.then_inc(s_pe, 1)
                elif kind == "gtr":
                    Wt(tensor, "dve", "scansB", NCH - 1)
                    gcol = g2.ap()[:, (NCH - 1) % 2, :, 127]
                    tensor.transpose(out=ps_xi.ap()[0:KI, 0:T],
                                     in_=gcol, identity=idf_sb.ap()[:]) \
                        .then_inc(s_pe, 1)
                elif kind == "gsel":
                    tensor.wait_ge(s_g, 48)
                    tensor.matmul(out=ps_xi.ap()[:, 2 * KI:3 * KI],
                                  lhsT=agt_sb.ap()[:], rhs=sel_sb.ap()[:],
                                  start=True, stop=True).then_inc(s_pe, 1)
                elif kind == "w0mm":
                    Wt(tensor, "dve", "vmul", 0)
                    last = None
                    for k in range(KI):
                        for n in range(2):
                            last = tensor.matmul(
                                out=ps_xi.ap()[0:1, 512 + n * 512:512 + (n + 1) * 512],
                                lhsT=vb_bf.ap()[:, k:k + 1],
                                rhs=wout_bf.ap()[:, k, n * 512:(n + 1) * 512],
                                start=(k == 0), stop=(k == KI - 1))
                    last.then_inc(s_pe, 1)
                elif kind == "w0bc":
                    Wt(tensor, "dve", "w0evac", 0)
                    Wt(tensor, "dve", "oev", NCH - 2)
                    last = None
                    for n in range(2):
                        last = tensor.matmul(
                            out=ps_aoA.ap()[:, n * 512:(n + 1) * 512],
                            lhsT=ones1_sb.ap()[:],
                            rhs=w0row.ap()[0:1, n * 512:(n + 1) * 512],
                            start=True, stop=True)
                    last.then_inc(s_pe, 1)
                elif kind in ("xiTA", "xiTB"):
                    half = 0 if kind == "xiTA" else 1
                    Wt(tensor, "act", "silu", c)
                    if half == 0:
                        Wt(tensor, "dve", "oev", c - 1)
                    xif = xif_ps(c)
                    last = None
                    for k in range(half * 8, half * 8 + 8):
                        last = tensor.transpose(
                            out=xif[:, k * T:(k + 1) * T],
                            in_=xi_tm(c)[:, k * T:(k + 1) * T],
                            identity=id_sb.ap()[:])
                    last.then_inc(s_pe, 1)

        block = stack.enter_context(nc.Block())

        @block.sync
        def _(sync):
            emit_sync(sync)

        @block.gpsimd
        def _(gpsimd):
            emit_pool(gpsimd)
            nf0 = 4 if use_cc else 0
            nf1 = 3 if use_cc else 0
            gpsimd.wait_ge(s_st[0], 16 * ((n_main + 1) // 2 + nf0))
            gpsimd.wait_ge(s_st[1], 16 * (n_main // 2 + nf1))

        @block.vector
        def _(vector):
            emit_dve(vector)

        @block.scalar
        def _(scalar):
            emit_act(scalar)

        @block.tensor
        def _(tensor):
            emit_pe(tensor)

    return nc


def make_inputs(x_flat, Win, bin_, Bp, Cp, Dp, Wout, bout, Wffn, disha,
                n_halo, n_main, n_cores, core_tokens, batch_tokens=16384):
    f32 = np.float32
    import ml_dtypes
    win_h = np.ascontiguousarray(Win.reshape(KD, T, DI), dtype=f32)
    wffT = np.ascontiguousarray(Wffn.T.reshape(KD, T, D), dtype=f32)
    dsh = np.ascontiguousarray(np.tile(disha, (D // disha.shape[0], 1))
                               .reshape(KD, T, D), dtype=f32)
    wout_h = np.ascontiguousarray(Wout.reshape(KI, T, D), dtype=f32)
    binb = np.ascontiguousarray(np.broadcast_to(bin_, (T, DI)), dtype=f32)
    boutb = np.ascontiguousarray(np.broadcast_to(bout, (T, D)), dtype=f32)
    bp_c = np.ascontiguousarray(Bp.reshape(KI, T, 16).transpose(1, 0, 2), dtype=f32)
    cpt_c = np.ascontiguousarray(Cp.T.reshape(KI, T, 16).transpose(1, 0, 2), dtype=f32)
    dp_c = np.ascontiguousarray(Dp.reshape(KI, T).T, dtype=f32)
    ident = np.eye(T, dtype=ml_dtypes.bfloat16)
    identf = np.eye(T, dtype=f32)
    ones1 = np.ones((1, T), dtype=ml_dtypes.bfloat16)
    fixco = np.ascontiguousarray(
        (0.9 ** (128.0 * np.arange(8)[None, :] + np.arange(T)[:, None] + 1.0))
        .astype(f32))
    halo_tok = n_halo * T
    in_maps = []
    for k in range(n_cores):
        s0 = k * core_tokens
        seq_start = (s0 % batch_tokens == 0)
        gate = np.zeros((T, 1), f32) if seq_start else np.ones((T, 1), f32)
        sel = np.zeros((T, KI), dtype=f32)
        if not seq_start:
            r = k - 1
            for kcol in range(KI):
                sel[r * KI + kcol, kcol] = 1.0
        if n_halo > 0:
            halo = (np.zeros((halo_tok, D), dtype=f32) if seq_start
                    else x_flat[s0 - halo_tok:s0])
            x_all = np.ascontiguousarray(
                np.concatenate([halo, x_flat[s0:s0 + core_tokens]], axis=0),
                dtype=f32)
        else:
            x_all = np.ascontiguousarray(x_flat[s0:s0 + core_tokens], dtype=f32)
        in_maps.append({
            "x_all": x_all, "win": win_h, "wffnT": wffT, "dishat": dsh,
            "woutw": wout_h, "bin_b": binb, "bout_b": boutb,
            "bp_cols": bp_c, "cpt_cols": cpt_c, "dp_col": dp_c,
            "gate": gate, "ident": ident, "identf": identf, "sel": sel,
            "fixco": fixco, "ones1": ones1,
        })
    return in_maps


_PROGRAM_CACHE = {}


def kernel(L0_attn_x, L0_attn_Win, L0_attn_bin_, L0_attn_Bp, L0_attn_Cp,
           L0_attn_Dp, L0_attn_Wout, L0_attn_bout, L0_ffn_weight,
           L0_ffn_disha):
    x = np.ascontiguousarray(np.asarray(L0_attn_x, dtype=np.float32)
                             .reshape(2 * 16384, D))
    core_tokens = N_MAIN * T
    in_maps = make_inputs(
        x, np.asarray(L0_attn_Win, dtype=np.float32),
        np.asarray(L0_attn_bin_, dtype=np.float32),
        np.asarray(L0_attn_Bp, dtype=np.float32),
        np.asarray(L0_attn_Cp, dtype=np.float32),
        np.asarray(L0_attn_Dp, dtype=np.float32),
        np.asarray(L0_attn_Wout, dtype=np.float32),
        np.asarray(L0_attn_bout, dtype=np.float32),
        np.asarray(L0_ffn_weight, dtype=np.float32),
        np.asarray(L0_ffn_disha, dtype=np.float32),
        N_HALO, N_MAIN, N_CORES, core_tokens)
    key = (N_HALO, N_MAIN)
    if key not in _PROGRAM_CACHE:
        _PROGRAM_CACHE[key] = build_program(N_HALO, N_MAIN)
    nc = _PROGRAM_CACHE[key]
    res = run_bass_kernel_spmd(nc, in_maps, core_ids=list(range(N_CORES)))
    out = np.concatenate([res.results[k]["out"] for k in range(N_CORES)],
                         axis=0)
    return out.reshape(2, 16384, D)


# version 35 (final)
# speedup vs baseline: 1.0830x; 1.0230x over previous
import numpy as np
import concourse.bass as bass
import concourse.mybir as mybir
from concourse.bass_utils import run_bass_kernel_spmd

F32 = mybir.dt.float32
BF16 = mybir.dt.bfloat16
AF = mybir.ActivationFunctionType
ALU = mybir.AluOpType
AX = mybir.AxisListType

D = 1024
DI = 2048
KD = D // 128
KI = DI // 128
T = 128
DECAY = 0.9
EPS = 1e-5

N_HALO = 1
N_MAIN = 32
N_CORES = 8


def build_program(n_halo, n_main):
    use_cc = (n_halo == 0)
    NCH = n_halo + n_main
    nc = bass.Bass("TRN2", target_bir_lowering=False)

    x_dr = nc.dram_tensor("x_all", [NCH * T, D], F32, kind="ExternalInput").ap() \
        .rearrange("(c p) d -> c p d", p=T)
    win_dr = nc.dram_tensor("win", [KD, T, DI], F32, kind="ExternalInput").ap()
    wff_dr = nc.dram_tensor("wffnT", [KD, T, D], F32, kind="ExternalInput").ap()
    dsh_dr = nc.dram_tensor("dishat", [KD, T, D], F32, kind="ExternalInput").ap()
    wout_dr = nc.dram_tensor("woutw", [KI, T, D], F32, kind="ExternalInput").ap()
    binb_dr = nc.dram_tensor("bin_b", [T, DI], F32, kind="ExternalInput").ap()
    boutb_dr = nc.dram_tensor("bout_b", [T, D], F32, kind="ExternalInput").ap()
    bp_dr = nc.dram_tensor("bp_cols", [T, KI, 16], F32, kind="ExternalInput").ap()
    cpt_dr = nc.dram_tensor("cpt_cols", [T, KI, 16], F32, kind="ExternalInput").ap()
    dp_dr = nc.dram_tensor("dp_col", [T, KI], F32, kind="ExternalInput").ap()
    gate_dr = nc.dram_tensor("gate", [T, 1], F32, kind="ExternalInput").ap()
    id_dr = nc.dram_tensor("ident", [T, T], BF16, kind="ExternalInput").ap()
    idf_dr = nc.dram_tensor("identf", [T, T], F32, kind="ExternalInput").ap()
    sel_dr = nc.dram_tensor("sel", [T, KI], F32, kind="ExternalInput").ap()
    fixco_dr = nc.dram_tensor("fixco", [T, 8], F32, kind="ExternalInput").ap()
    ones1_dr = nc.dram_tensor("ones1", [1, T], BF16, kind="ExternalInput").ap()
    g_dram = nc.dram_tensor("g_scratch", [KI, T], F32, kind="Internal")
    ag_dram = nc.dram_tensor("ag_scratch", [8 * KI, T], F32, kind="Internal")
    out_dr = nc.dram_tensor("out", [n_main * T, D], F32, kind="ExternalOutput").ap() \
        .rearrange("(c p) d -> c p d", p=T)

    from contextlib import ExitStack
    with ExitStack() as stack:
        sb = lambda n, s, d: stack.enter_context(nc.sbuf_tensor(n, s, d))
        psum = lambda n, s, d: stack.enter_context(nc.psum_tensor(n, s, d))
        sem = lambda n: stack.enter_context(nc.semaphore(n))

        win_bf = sb("win_bf", [T, KD, DI], BF16)
        wff_bf = sb("wff_bf", [T, KD, D], BF16)
        wout_bf = sb("wout_bf", [T, KI, D], BF16)
        binb = sb("binb", [T, DI], F32)
        boutb = sb("boutb", [T, D], F32)
        bp_sb = sb("bp_sb", [T, KI, 16], F32)
        cpt_sb = sb("cpt_sb", [T, KI, 16], F32)
        dp_sb = sb("dp_sb", [T, KI], F32)
        gate_sb = sb("gate_sb", [T, 1], F32)
        id_sb = sb("id_sb", [T, T], BF16)
        dec_sb = sb("dec_sb", [T, T], F32)
        c_col = sb("c_col", [T, KI], F32)
        stage = sb("stage", [T, 2, DI], F32)
        ginit = sb("ginit", [T, KI], F32)
        prodbig = u_sc_alias = None
        idf_sb = sb("idf_sb", [T, T], F32)
        sel_sb = sb("sel_sb", [T, KI], F32)
        fixco_sb = sb("fixco_sb", [T, 8], F32)
        ones1_sb = sb("ones1_sb", [1, T], BF16)
        gout_sb = sb("gout_sb", [KI, T], F32)
        agt_sb = sb("agt_sb", [T, T], F32)
        vb_bf = sb("vb_bf", [T, KI], BF16)
        w0row = sb("w0row", [1, D], BF16)
        x2 = sb("x2", [T, 2, D], F32)
        xn2 = sb("xn2", [T, 2, D], BF16)
        xnf2 = sb("xnf2", [T, 2, D], BF16)
        xit2 = sb("xit2", [T, 2, DI], BF16)
        u_sc = sb("u_sc", [T, DI], F32)
        bin_row = sb("bin_row", [1, DI], BF16)
        cg_sc = sb("cg_sc", [T, DI], F32)
        g2 = sb("g2", [T, 2, KI, T], F32)
        y2 = sb("y2", [T, 2, DI], BF16)
        out2 = sb("out2", [T, 2, D], F32)
        sq_a = sb("sq_a", [T, D], F32)
        stats = sb("stats", [T, 16], F32)

        prodbig = u_sc
        ps_xi = psum("ps_xi", [T, DI], F32)
        ps_aoA = psum("ps_aoA", [T, D], F32)
        ps_aoB = psum("ps_aoB", [T, D], F32)

        s_x = [sem("s_x0"), sem("s_x1")]
        s_sm = sem("s_sm")
        s_id = sem("s_id")
        s_wga = [sem("s_wga0"), sem("s_wga1"), sem("s_wga2"), sem("s_wga3")]
        s_wgb = sem("s_wgb")
        s_w = [sem("s_w0"), sem("s_w1"), sem("s_w2")]
        s_st = [sem("s_st0"), sem("s_st1")]
        s_cc = sem("s_cc")
        s_g = sem("s_g")
        s_f = [sem("s_f0"), sem("s_f1")]
        s_pe = sem("s_pe")
        s_dve = sem("s_dve")
        s_act = sem("s_act")
        s_pool = sem("s_pool")
        SEMS = {"pe": s_pe, "dve": s_dve, "act": s_act, "pool": s_pool}

        def wait_x(engine, c):
            engine.wait_ge(s_x[c % 2], 16 * (c // 2 + 1))

        def wait_st(engine, j):
            if j >= 0:
                engine.wait_ge(s_st[j % 2], 16 * (j // 2 + 1))

        xnf_ps = ps_xi.ap()[:, 0:512].bitcast(BF16)
        xif_psA = ps_aoA.ap().bitcast(BF16)
        xif_psB = ps_aoB.ap().bitcast(BF16)

        def ao_ps(c):
            return ps_aoA.ap() if c % 2 == 0 else ps_aoB.ap()

        def xif_ps(c):
            return xif_psB if c % 2 == 0 else xif_psA

        ismain = lambda c: c >= n_halo

        N_STAGE = 24

        def seq_sync():
            yield ("idload", 0)
            yield ("xload", 0)
            yield ("xload", 1)
            for i in range(10):
                yield ("smload", i)
            for s in range(8, 16):
                yield ("wload", s)
            for c in range(2, NCH):
                yield ("xload", c)

        def seq_pool():
            yield ("windma", 0)
            for d in range(8):
                yield ("ffnadd", d)
                if d >= 4:
                    yield ("woutdma", d - 4)
            if ismain(0):
                yield ("xpb", 0)
            for c in range(NCH):
                if ismain(c + 1) and c + 1 < NCH:
                    yield ("xpb", c + 1)
            if use_cc:
                yield ("gdma", 0)
                yield ("ccall", 0)
                yield ("agload", 0)

        def seq_dve():
            yield ("vhat", 0)
            yield ("xn", 0)
            yield ("ccolA", 0)
            yield ("ccolB", 0)
            yield ("binrow", 0)
            if use_cc:
                for d in range(8):
                    yield ("ffnadd", d)
            for c in range(NCH):
                if c + 1 < NCH:
                    yield ("vhat", c + 1)
                if ismain(c - 1):
                    yield ("oev", c - 1)
                if c + 1 < NCH:
                    yield ("xn", c + 1)
                yield ("scansA", c)
                if ismain(c):
                    yield ("cgvA", c)
                    yield ("yA", c)
                yield ("scansB", c)
                if c == n_halo - 1:
                    yield ("gateg", 0)
                if ismain(c):
                    yield ("cgvB", c)
                    yield ("yB", c)
            yield ("oev", NCH - 1)
            if use_cc:
                yield ("gevac", 0)
                yield ("vmul", 0)
                yield ("w0evac", 0)
                for j in range(7):
                    yield ("fix", j)

        def seq_act():
            yield ("stats", 0)
            yield ("rstd", 0)
            for c in range(NCH):
                yield ("xnfev", c)
                if c + 1 < NCH:
                    yield ("stats", c + 1)
                yield ("silu", c)
                if c + 1 < NCH:
                    yield ("rstd", c + 1)
                if use_cc and c == 1:
                    for i in range(8):
                        yield ("wpa", i)
                if ismain(c - 1):
                    yield ("store", c - 1)
            yield ("store", NCH - 1)
            if use_cc:
                for j in range(7):
                    yield ("fbload", j)
                    yield ("fstore", j)

        def seq_pe():
            yield ("xnT", 0)
            for c in range(NCH):
                yield ("win", c)
                if ismain(c - 1):
                    yield ("woutA", c - 1)
                    yield ("woutB", c - 1)
                if ismain(c):
                    yield ("wffn", c)
                yield ("xiTA", c)
                yield ("xiTB", c)
                if c + 1 < NCH:
                    yield ("xnT", c + 1)
            yield ("woutA", NCH - 1)
            yield ("woutB", NCH - 1)
            if use_cc:
                yield ("gtr", 0)
                yield ("gsel", 0)
                yield ("w0mm", 0)
                yield ("w0bc", 0)

        INCS = {"vhat": 5, "stats": 2, "vmul": 2, "silu": 3}
        ev = {}
        DMA_KINDS = {"store", "fbload", "fstore"}
        for eng, seqf in (("pe", seq_pe), ("dve", seq_dve), ("act", seq_act)):
            cnt = 0
            for ph in seqf():
                if ph[0] in DMA_KINDS:
                    continue
                cnt += INCS.get(ph[0], 1)
                ev[(eng,) + ph] = cnt
        pcnt = 0
        pool_before_xload = {}
        POOL_DMA_KINDS = {"xload", "gdma", "ccall", "agload", "windma", "woutdma"}
        for ph in seq_pool():
            if ph[0] not in POOL_DMA_KINDS:
                pcnt += INCS.get(ph[0], 1)
                ev[("pool",) + ph] = pcnt
            elif ph[0] == "xload":
                pool_before_xload[ph[1]] = pcnt
        ev_w = {}
        wcnt = [0, 0]
        for ph in seq_sync():
            if ph[0] == "wload":
                s = ph[1]
                wcnt[s % 2] += 32
                ev_w[s] = (s % 2, wcnt[s % 2])
        N_SMALL = 160

        def wait_w(engine, s):
            p, v = ev_w[s]
            engine.wait_ge(s_w[p], v)

        def Wt(engine, eng, *key):
            k = (eng,) + key
            if k in ev:
                engine.wait_ge(SEMS[eng], ev[k])

        def selfwait(engine, eng, ph):
            v = ev[(eng,) + ph] - INCS.get(ph[0], 1)
            if v > 0:
                engine.wait_ge(SEMS[eng], v)

        x_tm = lambda c: x2.ap()[:, c % 2, :]
        xpb_tm = lambda c: out2.ap()[:, c % 2, :]
        xn_tm = lambda c: xn2.ap()[:, c % 2, :]
        xnf_sb = lambda c: xnf2.ap()[:, c % 2, :]
        xi_tm = lambda c: xit2.ap()[:, c % 2, :]
        g_sb = lambda c: g2.ap()[:, c % 2, :, :]
        y_sb = lambda c: y2.ap()[:, c % 2, :]
        out_sb = lambda c: out2.ap()[:, c % 2, :]
        stat = lambda c, j: stats.ap()[:, (c % 2) * 8 + j:(c % 2) * 8 + j + 1]

        def emit_sync(sync):
            small = [(binb, binb_dr), (boutb, boutb_dr), (bp_sb, bp_dr),
                     (cpt_sb, cpt_dr), (dp_sb, dp_dr), (gate_sb, gate_dr),
                     (idf_sb, idf_dr), (sel_sb, sel_dr), (fixco_sb, fixco_dr),
                     (ones1_sb, ones1_dr)]
            for ph in seq_sync():
                if ph[0] == "idload":
                    sync.dma_start(out=id_sb.ap()[:], in_=id_dr).then_inc(s_id, 16)
                    continue
                if ph[0] == "smload":
                    dst, src = small[ph[1]]
                    sync.dma_start(out=dst.ap()[:], in_=src).then_inc(s_sm, 16)
                    continue
                if ph[0] == "xload":
                    c = ph[1]
                    if (c - 2) >= 0:
                        Wt(sync, "act", "stats", c - 2)
                        Wt(sync, "dve", "xn", c - 2)
                        Wt(sync, "pool", "xpb", c - 2)
                    sync.dma_start(out=x_tm(c),
                                   in_=x_dr[c]).then_inc(s_x[c % 2], 16)
                    continue
                s = ph[1]
                if s == 8:
                    for q in range(4):
                        sync.wait_ge(s_wga[q], 16)
                p = s - 2
                if 8 <= p < 16:
                    Wt(sync, "dve" if use_cc else "pool", "ffnadd", p - 8)
                buf = stage.ap()[:, s % 2, :]
                sw = s_w[s % 2]
                d = s - 8
                sync.dma_start(out=buf[:, 0:D],
                               in_=wff_dr[d]).then_inc(sw, 16)
                sync.dma_start(out=buf[:, D:2 * D],
                               in_=dsh_dr[d]).then_inc(sw, 16)

        def emit_pool(gpsimd):
            for ph in seq_pool():
                kind, c = ph
                if kind == "windma":
                    wv = win_dr.rearrange("d p n -> p d n")
                    for q in range(4):
                        gpsimd.dma_start(
                            out=win_bf.ap()[:, 2 * q:2 * q + 2, :],
                            in_=wv[:, 2 * q:2 * q + 2, :]).then_inc(s_wga[q], 16)
                elif kind == "woutdma":
                    wv = wout_dr.rearrange("d p n -> p d n")
                    q = c
                    gpsimd.dma_start(
                        out=wout_bf.ap()[:, 4 * q:4 * q + 4, :],
                        in_=wv[:, 4 * q:4 * q + 4, :]).then_inc(s_wgb, 16)
                elif kind == "xpb":
                    selfwait(gpsimd, "pool", ph)
                    wait_x(gpsimd, c)
                    if c == n_halo:
                        gpsimd.wait_ge(s_sm, N_SMALL)
                    wait_st(gpsimd, c - n_halo - 2)
                    Wt(gpsimd, "dve", "oev", c - 2)
                    gpsimd.tensor_tensor(out=xpb_tm(c), in0=x_tm(c),
                                         in1=boutb.ap()[:],
                                         op=ALU.add).then_inc(s_pool, 1)
                elif kind == "ffnadd":
                    selfwait(gpsimd, "pool", ph)
                    d = c
                    wait_w(gpsimd, 8 + d)
                    buf = stage.ap()[:, d % 2, :]
                    gpsimd.tensor_tensor(out=wff_bf.ap()[:, d, :],
                                         in0=buf[:, 0:D], in1=buf[:, D:2 * D],
                                         op=ALU.add).then_inc(s_pool, 1)
                elif kind == "gdma":
                    Wt(gpsimd, "dve", "gevac", 0)
                    gpsimd.dma_start(out=g_dram.ap(), in_=gout_sb.ap()[:]) \
                        .then_inc(s_g, 16)
                elif kind == "ccall":
                    gpsimd.wait_ge(s_g, 16)
                    gpsimd.collective_compute(
                        "AllGather", ALU.bypass,
                        replica_groups=[list(range(N_CORES))],
                        ins=[g_dram.ap().opt()],
                        outs=[ag_dram.ap().opt()]).then_inc(s_cc, 1)
                elif kind == "agload":
                    gpsimd.wait_ge(s_cc, 1)
                    gpsimd.dma_start(out=agt_sb.ap()[:], in_=ag_dram.ap()) \
                        .then_inc(s_g, 32)

        def emit_dve(vector):
            vector.memset(dec_sb.ap()[:], DECAY)
            for ph in seq_dve():
                kind, c = ph
                selfwait(vector, "dve", ph)
                if kind == "ccolA":
                    vector.wait_ge(s_sm, N_SMALL)
                    vector.tensor_tensor(
                        out=prodbig.ap()[:, 0:KI * 16],
                        in0=bp_sb.ap()[:].rearrange("p a b -> p (a b)"),
                        in1=cpt_sb.ap()[:].rearrange("p a b -> p (a b)"),
                        op=ALU.mult).then_inc(s_dve, 1)
                elif kind == "ccolB":
                    last = None
                    for k in range(KI):
                        last = vector.tensor_reduce(
                            out=c_col.ap()[:, k:k + 1],
                            in_=prodbig.ap()[:, k * 16:(k + 1) * 16],
                            axis=AX.X, op=ALU.add)
                    last.then_inc(s_dve, 1)
                elif kind == "ffnadd":
                    d = c
                    wait_w(vector, 8 + d)
                    buf = stage.ap()[:, d % 2, :]
                    vector.tensor_tensor(out=wff_bf.ap()[:, d, :],
                                         in0=buf[:, 0:D], in1=buf[:, D:2 * D],
                                         op=ALU.add).then_inc(s_dve, 1)
                elif kind == "gevac":
                    Wt(vector, "pe", "gtr", 0)
                    vector.tensor_copy(out=gout_sb.ap()[:],
                                       in_=ps_xi.ap()[0:KI, 0:T]) \
                        .then_inc(s_dve, 1)
                elif kind == "vmul":
                    base = ev[("dve",) + ph] - 2
                    Wt(vector, "pe", "gsel", 0)
                    vector.tensor_tensor(out=ginit.ap()[:],
                                         in0=ps_xi.ap()[:, 2 * KI:3 * KI],
                                         in1=c_col.ap()[:],
                                         op=ALU.mult).then_inc(s_dve, 1)
                    vector.wait_ge(s_dve, base + 1)
                    vector.tensor_copy(out=vb_bf.ap()[:],
                                       in_=ginit.ap()[:]).then_inc(s_dve, 1)
                elif kind == "w0evac":
                    Wt(vector, "pe", "w0mm", 0)
                    vector.tensor_copy(out=w0row.ap()[:],
                                       in_=ps_xi.ap()[0:1, 512:512 + D]) \
                        .then_inc(s_dve, 1)
                elif kind == "fix":
                    j = c
                    Wt(vector, "pe", "w0bc", 0)
                    vector.wait_ge(s_f[j % 2], 16 * (j // 2 + 1))
                    vector.scalar_tensor_tensor(
                        out=out2.ap()[:, j % 2, :],
                        in0=ps_aoA.ap()[:],
                        scalar=fixco_sb.ap()[:, j:j + 1],
                        in1=out2.ap()[:, j % 2, :],
                        op0=ALU.mult, op1=ALU.add).then_inc(s_dve, 1)
                elif kind == "binrow":
                    vector.wait_ge(s_sm, N_SMALL)
                    vector.tensor_copy(out=bin_row.ap()[:],
                                       in_=binb.ap()[0:1, :]).then_inc(s_dve, 1)
                elif kind == "vhat":
                    base = ev[("dve",) + ph] - 5
                    Wt(vector, "act", "stats", c)
                    vector.tensor_scalar(out=stat(c, 2), in0=stat(c, 0),
                                         scalar1=-1.0 / D, scalar2=None,
                                         op0=ALU.mult).then_inc(s_dve, 1)
                    vector.tensor_scalar(out=stat(c, 3), in0=stat(c, 1),
                                         scalar1=1.0 / D, scalar2=EPS,
                                         op0=ALU.mult,
                                         op1=ALU.add).then_inc(s_dve, 1)
                    vector.wait_ge(s_dve, base + 2)
                    vector.tensor_tensor(out=stat(c, 4), in0=stat(c, 2),
                                         in1=stat(c, 2),
                                         op=ALU.mult).then_inc(s_dve, 1)
                    vector.wait_ge(s_dve, base + 3)
                    vector.tensor_tensor(out=stat(c, 5), in0=stat(c, 3),
                                         in1=stat(c, 4),
                                         op=ALU.subtract).then_inc(s_dve, 1)
                    vector.wait_ge(s_dve, base + 4)
                    vector.reciprocal(out=stat(c, 6), in_=stat(c, 5)) \
                        .then_inc(s_dve, 1)
                elif kind == "xn":
                    Wt(vector, "act", "rstd", c)
                    Wt(vector, "pe", "xnT", c - 2)
                    vector.tensor_scalar(out=xn_tm(c), in0=x_tm(c),
                                         scalar1=stat(c, 2), scalar2=stat(c, 7),
                                         op0=ALU.add, op1=ALU.mult) \
                        .then_inc(s_dve, 1)
                elif kind == "oev":
                    Wt(vector, "pe", "woutB", c)
                    Wt(vector, "pool", "xpb", c)
                    vector.tensor_tensor(out=out_sb(c), in0=ao_ps(c)[:],
                                         in1=xpb_tm(c), op=ALU.add) \
                        .then_inc(s_dve, 1)
                elif kind in ("scansA", "scansB"):
                    half = 0 if kind == "scansA" else 1
                    Wt(vector, "pe", "xiTA" if half == 0 else "xiTB", c)
                    xif = xif_ps(c)
                    last = None
                    for k in range(half * 8, half * 8 + 8):
                        if c == 0:
                            init = 0.0
                        elif c == n_halo:
                            init = ginit.ap()[:, k:k + 1]
                        else:
                            init = g_sb(c - 1)[:, k, 127:128]
                        last = vector.tensor_tensor_scan(
                            out=g_sb(c)[:, k, :], data0=dec_sb.ap()[:],
                            data1=xif[:, k * T:(k + 1) * T], initial=init,
                            op0=ALU.mult, op1=ALU.add)
                    last.then_inc(s_dve, 1)
                elif kind == "gateg":
                    h = n_halo - 1
                    vector.tensor_scalar_mul(
                        out=ginit.ap()[:, :],
                        in0=g2.ap()[:, h % 2, :, 127],
                        scalar1=gate_sb.ap()[:, 0:1]).then_inc(s_dve, 1)
                elif kind in ("cgvA", "cgvB"):
                    half = 0 if kind == "cgvA" else 1
                    last = None
                    for k in range(half * 8, half * 8 + 8):
                        last = vector.tensor_scalar_mul(
                            out=cg_sc.ap()[:, k * T:(k + 1) * T],
                            in0=g_sb(c)[:, k, :],
                            scalar1=c_col.ap()[:, k:k + 1])
                    last.then_inc(s_dve, 1)
                elif kind in ("yA", "yB"):
                    half = 0 if kind == "yA" else 1
                    Wt(vector, "pe", "woutA" if half == 0 else "woutB", c - 2)
                    xif = xif_ps(c)
                    last = None
                    for k in range(half * 8, half * 8 + 8):
                        last = vector.scalar_tensor_tensor(
                            out=y_sb(c)[:, k * T:(k + 1) * T],
                            in0=xif[:, k * T:(k + 1) * T],
                            scalar=dp_sb.ap()[:, k:k + 1],
                            in1=cg_sc.ap()[:, k * T:(k + 1) * T],
                            op0=ALU.mult, op1=ALU.add)
                    last.then_inc(s_dve, 1)

        def emit_act(scalar):
            for ph in seq_act():
                kind, c = ph
                if kind not in DMA_KINDS:
                    selfwait(scalar, "act", ph)
                if kind == "wpa":
                    i = c
                    wait_w(scalar, 16 + i)
                    buf = stage.ap()[:, (16 + i) % 3, :]
                    scalar.copy(out=wout_bf.ap()[:, i, :], in_=buf[:, 0:D])
                    scalar.copy(out=wout_bf.ap()[:, i + 8, :],
                                in_=buf[:, D:2 * D]).then_inc(s_act, 1)
                elif kind == "fbload":
                    j = c
                    last = n_main - 1 - ((n_main - 1 - (j % 2)) % 2)
                    scalar.wait_ge(s_st[j % 2], 16 * (last // 2 + 1 + j // 2))
                    if j >= 2:
                        Wt(scalar, "dve", "fix", j - 2)
                    scalar.dma_start(out=out2.ap()[:, j % 2, :],
                                     in_=out_dr[j]).then_inc(s_f[j % 2], 16)
                elif kind == "fstore":
                    j = c
                    Wt(scalar, "dve", "fix", j)
                    scalar.dma_start(out=out_dr[j],
                                     in_=out2.ap()[:, j % 2, :]) \
                        .then_inc(s_st[j % 2], 16)
                elif kind == "stats":
                    base = ev[("act",) + ph] - 2
                    wait_x(scalar, c)
                    Wt(scalar, "dve", "vhat", c - 2)
                    scalar.activation(out=sq_a.ap()[:], in_=x_tm(c),
                                      func=AF.Identity,
                                      accum_out=stat(c, 0)).then_inc(s_act, 1)
                    scalar.wait_ge(s_act, base + 1)
                    scalar.activation(out=sq_a.ap()[:], in_=x_tm(c),
                                      func=AF.Square, accum_out=stat(c, 1)) \
                        .then_inc(s_act, 1)
                elif kind == "rstd":
                    Wt(scalar, "dve", "vhat", c)
                    Wt(scalar, "dve", "xn", c - 2)
                    scalar.activation(out=stat(c, 7), in_=stat(c, 6),
                                      func=AF.Sqrt).then_inc(s_act, 1)
                elif kind == "xnfev":
                    Wt(scalar, "pe", "xnT", c)
                    Wt(scalar, "pe", "win", c - 2)
                    scalar.copy(out=xnf_sb(c), in_=xnf_ps[:, 0:D]) \
                        .then_inc(s_act, 1)
                elif kind == "silu":
                    scalar.activation(out=sq_a.ap()[:, 2:3],
                                      in_=sq_a.ap()[:, 0:1],
                                      func=AF.Silu).then_inc(s_act, 1)
                    Wt(scalar, "pe", "win", c)
                    Wt(scalar, "pe", "xiTB", c - 2)
                    scalar.activation(out=xi_tm(c), in_=ps_xi.ap()[:],
                                      func=AF.Silu).then_inc(s_act, 1)
                    scalar.activation(out=sq_a.ap()[:, 3:4],
                                      in_=sq_a.ap()[:, 0:1],
                                      func=AF.Sqrt).then_inc(s_act, 1)
                elif kind == "store":
                    Wt(scalar, "dve", "oev", c)
                    j = c - n_halo
                    scalar.dma_start(out=out_dr[j], in_=out_sb(c)) \
                        .then_inc(s_st[j % 2], 16)

        def emit_pe(tensor):
            tensor.wait_ge(s_id, 16)
            for q in range(4):
                tensor.wait_ge(s_wga[q], 16)
                for _ in range(25):
                    tensor.transpose(out=xif_psA[:, 0:T], in_=id_sb.ap()[:],
                                     identity=id_sb.ap()[:])
            for ph in seq_pe():
                kind, c = ph
                if kind == "xnT":
                    if c == 0:
                        tensor.wait_ge(s_id, 16)
                    Wt(tensor, "act", "silu", c - 1)
                    Wt(tensor, "dve", "xn", c)
                    last = None
                    for d in range(KD):
                        last = tensor.transpose(
                            out=xnf_ps[:, d * T:(d + 1) * T],
                            in_=xn_tm(c)[:, d * T:(d + 1) * T],
                            identity=id_sb.ap()[:])
                    last.then_inc(s_pe, 1)
                elif kind == "win":
                    Wt(tensor, "act", "xnfev", c)
                    if c == 0:
                        for q in range(4):
                            tensor.wait_ge(s_wga[q], 16)
                        Wt(tensor, "dve", "binrow", 0)
                    last = None
                    for d in range(KD):
                        lhsT = xnf_sb(c)[:, d * T:(d + 1) * T]
                        for n in range(4):
                            last = tensor.matmul(
                                out=ps_xi.ap()[:, n * 512:(n + 1) * 512],
                                lhsT=lhsT,
                                rhs=win_bf.ap()[:, d, n * 512:(n + 1) * 512],
                                start=(d == 0), stop=False,
                                skip_group_check=True)
                    for n in range(4):
                        last = tensor.matmul(
                            out=ps_xi.ap()[:, n * 512:(n + 1) * 512],
                            lhsT=ones1_sb.ap()[0:1, :],
                            rhs=bin_row.ap()[0:1, n * 512:(n + 1) * 512],
                            start=False, stop=True, skip_group_check=True)
                    last.then_inc(s_pe, 1)
                elif kind == "wffn":
                    Wt(tensor, "dve", "yB", c - 1)
                    if c == n_halo:
                        Wt(tensor, "dve", "scansB", c - 1)
                        Wt(tensor, "dve" if use_cc else "pool",
                           "ffnadd", KD - 1)
                    last = None
                    for d in range(KD):
                        lhsT = xnf_sb(c)[:, d * T:(d + 1) * T]
                        for n in range(2):
                            last = tensor.matmul(
                                out=ao_ps(c)[:, n * 512:(n + 1) * 512],
                                lhsT=lhsT,
                                rhs=wff_bf.ap()[:, d, n * 512:(n + 1) * 512],
                                start=(d == 0), stop=False,
                                skip_group_check=True)
                    last.then_inc(s_pe, 1)
                elif kind in ("woutA", "woutB"):
                    half = 0 if kind == "woutA" else 1
                    Wt(tensor, "dve", "yA" if half == 0 else "yB", c)
                    if c == n_halo and half == 0:
                        if use_cc:
                            Wt(tensor, "act", "wpa", 7)
                        else:
                            tensor.wait_ge(s_wgb, 64)
                    last = None
                    for k in range(half * 8, half * 8 + 8):
                        lhsT = y_sb(c)[:, k * T:(k + 1) * T]
                        for n in range(2):
                            last = tensor.matmul(
                                out=ao_ps(c)[:, n * 512:(n + 1) * 512],
                                lhsT=lhsT,
                                rhs=wout_bf.ap()[:, k, n * 512:(n + 1) * 512],
                                start=False, stop=(k == KI - 1),
                                skip_group_check=True)
                    last.then_inc(s_pe, 1)
                elif kind == "gtr":
                    Wt(tensor, "dve", "scansB", NCH - 1)
                    gcol = g2.ap()[:, (NCH - 1) % 2, :, 127]
                    tensor.transpose(out=ps_xi.ap()[0:KI, 0:T],
                                     in_=gcol, identity=idf_sb.ap()[:]) \
                        .then_inc(s_pe, 1)
                elif kind == "gsel":
                    tensor.wait_ge(s_g, 48)
                    tensor.matmul(out=ps_xi.ap()[:, 2 * KI:3 * KI],
                                  lhsT=agt_sb.ap()[:], rhs=sel_sb.ap()[:],
                                  start=True, stop=True).then_inc(s_pe, 1)
                elif kind == "w0mm":
                    Wt(tensor, "dve", "vmul", 0)
                    last = None
                    for k in range(KI):
                        for n in range(2):
                            last = tensor.matmul(
                                out=ps_xi.ap()[0:1, 512 + n * 512:512 + (n + 1) * 512],
                                lhsT=vb_bf.ap()[:, k:k + 1],
                                rhs=wout_bf.ap()[:, k, n * 512:(n + 1) * 512],
                                start=(k == 0), stop=(k == KI - 1))
                    last.then_inc(s_pe, 1)
                elif kind == "w0bc":
                    Wt(tensor, "dve", "w0evac", 0)
                    Wt(tensor, "dve", "oev", NCH - 2)
                    last = None
                    for n in range(2):
                        last = tensor.matmul(
                            out=ps_aoA.ap()[:, n * 512:(n + 1) * 512],
                            lhsT=ones1_sb.ap()[:],
                            rhs=w0row.ap()[0:1, n * 512:(n + 1) * 512],
                            start=True, stop=True)
                    last.then_inc(s_pe, 1)
                elif kind in ("xiTA", "xiTB"):
                    half = 0 if kind == "xiTA" else 1
                    Wt(tensor, "act", "silu", c)
                    if half == 0:
                        Wt(tensor, "dve", "oev", c - 1)
                    xif = xif_ps(c)
                    last = None
                    for k in range(half * 8, half * 8 + 8):
                        last = tensor.transpose(
                            out=xif[:, k * T:(k + 1) * T],
                            in_=xi_tm(c)[:, k * T:(k + 1) * T],
                            identity=id_sb.ap()[:])
                    last.then_inc(s_pe, 1)

        block = stack.enter_context(nc.Block())

        @block.sync
        def _(sync):
            emit_sync(sync)

        @block.gpsimd
        def _(gpsimd):
            emit_pool(gpsimd)
            nf0 = 4 if use_cc else 0
            nf1 = 3 if use_cc else 0
            gpsimd.wait_ge(s_st[0], 16 * ((n_main + 1) // 2 + nf0))
            gpsimd.wait_ge(s_st[1], 16 * (n_main // 2 + nf1))

        @block.vector
        def _(vector):
            emit_dve(vector)

        @block.scalar
        def _(scalar):
            emit_act(scalar)

        @block.tensor
        def _(tensor):
            emit_pe(tensor)

    return nc


def make_inputs(x_flat, Win, bin_, Bp, Cp, Dp, Wout, bout, Wffn, disha,
                n_halo, n_main, n_cores, core_tokens, batch_tokens=16384):
    f32 = np.float32
    import ml_dtypes
    win_h = np.ascontiguousarray(Win.reshape(KD, T, DI), dtype=f32)
    wffT = np.ascontiguousarray(Wffn.T.reshape(KD, T, D), dtype=f32)
    dsh = np.ascontiguousarray(np.tile(disha, (D // disha.shape[0], 1))
                               .reshape(KD, T, D), dtype=f32)
    wout_h = np.ascontiguousarray(Wout.reshape(KI, T, D), dtype=f32)
    binb = np.ascontiguousarray(np.broadcast_to(bin_, (T, DI)), dtype=f32)
    boutb = np.ascontiguousarray(np.broadcast_to(bout, (T, D)), dtype=f32)
    bp_c = np.ascontiguousarray(Bp.reshape(KI, T, 16).transpose(1, 0, 2), dtype=f32)
    cpt_c = np.ascontiguousarray(Cp.T.reshape(KI, T, 16).transpose(1, 0, 2), dtype=f32)
    dp_c = np.ascontiguousarray(Dp.reshape(KI, T).T, dtype=f32)
    ident = np.eye(T, dtype=ml_dtypes.bfloat16)
    identf = np.eye(T, dtype=f32)
    ones1 = np.ones((1, T), dtype=ml_dtypes.bfloat16)
    fixco = np.ascontiguousarray(
        (0.9 ** (128.0 * np.arange(8)[None, :] + np.arange(T)[:, None] + 1.0))
        .astype(f32))
    halo_tok = n_halo * T
    in_maps = []
    for k in range(n_cores):
        s0 = k * core_tokens
        seq_start = (s0 % batch_tokens == 0)
        gate = np.zeros((T, 1), f32) if seq_start else np.ones((T, 1), f32)
        sel = np.zeros((T, KI), dtype=f32)
        if not seq_start:
            r = k - 1
            for kcol in range(KI):
                sel[r * KI + kcol, kcol] = 1.0
        if n_halo > 0:
            halo = (np.zeros((halo_tok, D), dtype=f32) if seq_start
                    else x_flat[s0 - halo_tok:s0])
            x_all = np.ascontiguousarray(
                np.concatenate([halo, x_flat[s0:s0 + core_tokens]], axis=0),
                dtype=f32)
        else:
            x_all = np.ascontiguousarray(x_flat[s0:s0 + core_tokens], dtype=f32)
        in_maps.append({
            "x_all": x_all, "win": win_h, "wffnT": wffT, "dishat": dsh,
            "woutw": wout_h, "bin_b": binb, "bout_b": boutb,
            "bp_cols": bp_c, "cpt_cols": cpt_c, "dp_col": dp_c,
            "gate": gate, "ident": ident, "identf": identf, "sel": sel,
            "fixco": fixco, "ones1": ones1,
        })
    return in_maps


_PROGRAM_CACHE = {}


def kernel(L0_attn_x, L0_attn_Win, L0_attn_bin_, L0_attn_Bp, L0_attn_Cp,
           L0_attn_Dp, L0_attn_Wout, L0_attn_bout, L0_ffn_weight,
           L0_ffn_disha):
    x = np.ascontiguousarray(np.asarray(L0_attn_x, dtype=np.float32)
                             .reshape(2 * 16384, D))
    core_tokens = N_MAIN * T
    in_maps = make_inputs(
        x, np.asarray(L0_attn_Win, dtype=np.float32),
        np.asarray(L0_attn_bin_, dtype=np.float32),
        np.asarray(L0_attn_Bp, dtype=np.float32),
        np.asarray(L0_attn_Cp, dtype=np.float32),
        np.asarray(L0_attn_Dp, dtype=np.float32),
        np.asarray(L0_attn_Wout, dtype=np.float32),
        np.asarray(L0_attn_bout, dtype=np.float32),
        np.asarray(L0_ffn_weight, dtype=np.float32),
        np.asarray(L0_ffn_disha, dtype=np.float32),
        N_HALO, N_MAIN, N_CORES, core_tokens)
    key = (N_HALO, N_MAIN)
    if key not in _PROGRAM_CACHE:
        _PROGRAM_CACHE[key] = build_program(N_HALO, N_MAIN)
    nc = _PROGRAM_CACHE[key]
    res = run_bass_kernel_spmd(nc, in_maps, core_ids=list(range(N_CORES)))
    out = np.concatenate([res.results[k]["out"] for k in range(N_CORES)],
                         axis=0)
    return out.reshape(2, 16384, D)
